# version 11
# baseline (speedup 1.0000x reference)
import math
import numpy as np
import ml_dtypes
import concourse.bass as bass
import concourse.mybir as mybir
from concourse.bass_utils import run_bass_kernel_spmd
from concourse.aot_env import allow_ici_all_to_all
from contextlib import ExitStack

F32 = mybir.dt.float32
BF16 = mybir.dt.bfloat16
AF = mybir.ActivationFunctionType
ALU = mybir.AluOpType
AX = mybir.AxisListType.X

NCORES = 8
B, S, D = 4, 2048, 4096
SCH = S // NCORES
TLOC = B * SCH
NT = D // 128
HLOC = 4
NOPE, ROPE, VD, QHD, KVR, QLAT = 64, 64, 128, 128, 256, 1024
Q_START, Q_END = 611, 2047
LATR = QLAT + KVR + ROPE

C_NEG30, C_SMS, C_L0SC, C_TWO, C_EPS5, C_EPS6, C_I4096, C_IPOOL, C_I1024, C_I256, C_ZERO = range(11)
NCONST = 16

STAGE = 4
DBG = None



def _yarn_cos_sin(seq_len, dim=64, base=10000.0, factor=128.0, beta_fast=32.0,
                  beta_slow=1.0, orig_max=8192, mscale=1.0, mscale_all_dim=1.0):
    def corr_dim(r):
        return dim * math.log(orig_max / (r * 2 * math.pi)) / (2 * math.log(base))
    low = max(math.floor(corr_dim(beta_fast)), 0)
    high = min(math.ceil(corr_dim(beta_slow)), dim - 1)
    hi = high + 0.001 if low == high else float(high)
    ramp = np.clip((np.arange(dim // 2, dtype=np.float32) - low) / (hi - low), 0.0, 1.0)
    inv_freq_mask = 1.0 - ramp
    freq_extra = 1.0 / base ** (np.arange(0, dim, 2, dtype=np.float32) / dim)
    freq_inter = freq_extra / factor
    inv_freq = freq_inter * (1.0 - inv_freq_mask) + freq_extra * inv_freq_mask
    t = np.arange(seq_len, dtype=np.float32)
    freqs = np.outer(t, inv_freq)
    emb = np.concatenate([freqs, freqs], axis=-1)

    def gm(s, m):
        return 1.0 if s <= 1 else 0.1 * m * math.log(s) + 1.0
    ms = gm(factor, mscale) / gm(factor, mscale_all_dim)
    return np.cos(emb) * ms, np.sin(emb) * ms


def _deint(lo, n=64):
    return [lo + 2 * i for i in range(n // 2)] + [lo + 1 + 2 * i for i in range(n // 2)]


def _prep_host(inputs):
    f = lambda k: np.asarray(inputs[k], dtype=np.float32)
    x_full = f("L0_attn_x")

    pieces = [[], []]
    offs = {}

    def add(wid, name, arr):
        arr = np.ascontiguousarray(arr, dtype=np.float32)
        offs[name] = (wid, sum(p.size for p in pieces[wid]), arr.shape)
        pieces[wid].append(arr.reshape(-1))

    add(0, "W0T", f("L0_ffn_W_base").T)
    add(0, "G0T", f("L0_ffn_W_gate").T)
    add(0, "A0T", f("L0_ffn_A").transpose(2, 0, 1).reshape(D, 128))
    add(0, "B0S", f("L0_ffn_B").transpose(0, 2, 1).reshape(128, D))
    add(1, "QAT", f("L1_attn_wq_a").T)
    wkv = f("L1_attn_wkv_a")
    add(1, "KAT", np.concatenate([wkv[0:KVR], wkv[_deint(KVR)]], axis=0).T)
    add(1, "WOT", f("L1_attn_wo").T)
    add(1, "W1T", f("L1_ffn_W").T)
    add(1, "A1T", f("L1_ffn_A").transpose(2, 0, 1).reshape(D, 128))
    add(1, "B1S", f("L1_ffn_B").transpose(0, 2, 1).reshape(128, D))
    add(1, "RWT", f("L1_ffn_router_W").T)
    walls, wps = [], []
    for wid in range(2):
        wall = np.concatenate(pieces[wid])
        lp = ((wall.size + 1024 * NCORES - 1) // (1024 * NCORES)) * (1024 * NCORES)
        walls.append(np.pad(wall, (0, lp - wall.size)))
        wps.append(lp // NCORES // 128)

    wqbm = f("L1_attn_wq_b")
    wkvb = f("L1_attn_wkv_b")
    per_core_w = []
    for c in range(NCORES):
        qrows, nrows, vrows = [], [], []
        for hl in range(HLOC):
            h = HLOC * c + hl
            qrows += list(range(h * QHD, h * QHD + NOPE)) + _deint(h * QHD + NOPE)
            nrows += list(range(h * 192, h * 192 + NOPE))
            vrows += list(range(h * 192 + NOPE, h * 192 + 192))
        per_core_w.append((np.ascontiguousarray(wqbm[qrows].T),
                           np.ascontiguousarray(wkvb[nrows].T),
                           np.ascontiguousarray(wkvb[vrows].T)))

    bvec = np.zeros((128, 64), np.float32)
    bvec[:, 0:32] = f("L0_ffn_b_base").reshape(32, 128).T
    bvec[:, 32:64] = f("L1_ffn_b").reshape(32, 128).T
    lnw = np.zeros((128, 16), np.float32)
    lnw[:, 0:8] = f("L1_attn_q_a_ln_w").reshape(8, 128).T
    lnw[:, 8:10] = f("L1_attn_kv_a_ln_w").reshape(2, 128).T
    rb = f("L1_ffn_router_b").reshape(8, 1)

    m_ = 0.1 * 1.0 * math.log(128.0) + 1.0
    consts = np.zeros((128, NCONST), np.float32)
    for i, v in [(C_NEG30, -30.0), (C_SMS, (QHD ** -0.5) * m_ * m_),
                 (C_L0SC, 1.0 / math.sqrt(2048.0)), (C_TWO, 2.0),
                 (C_EPS5, 1e-5), (C_EPS6, 1e-6), (C_I4096, 1.0 / 4096.0),
                 (C_IPOOL, 1.0 / (Q_END - Q_START)), (C_I1024, 1.0 / 1024.0),
                 (C_I256, 1.0 / 256.0), (C_ZERO, 0.0)]:
        consts[:, i] = v

    cosf, sinf = _yarn_cos_sin(S)
    CS2 = np.zeros((128, S), np.float32)
    CS2[0:32] = cosf.T[0:32]
    CS2[32:64] = cosf.T[32:64]
    CS2[64:96] = sinf.T[0:32]
    CS2[96:128] = sinf.T[32:64]

    identf = np.eye(128, dtype=np.float32)
    trib = np.triu(np.ones((128, 128), np.float32)).astype(ml_dtypes.bfloat16)
    expf = np.zeros((8, 128), np.float32)
    for e in range(8):
        expf[e, e * 16:(e + 1) * 16] = 1.0
    ones1f = np.ones((1, 128), np.float32)
    ones3f = np.ones((96, 128), np.float32)
    onesb = np.ones((128, 1), ml_dtypes.bfloat16)
    onesf = np.ones((128, 1), np.float32)

    in_maps = []
    for c in range(NCORES):
        s0 = SCH * c
        x_c = np.ascontiguousarray(x_full[:, s0:s0 + SCH, :].reshape(TLOC, D).T)
        wallpA = np.ascontiguousarray(
            walls[0][c * 128 * wps[0]:(c + 1) * 128 * wps[0]].reshape(128, wps[0]))
        wallpB = np.ascontiguousarray(
            walls[1][c * 128 * wps[1]:(c + 1) * 128 * wps[1]].reshape(128, wps[1]))
        mask = np.zeros((128, SCH), np.float32)
        for ds in range(SCH):
            if Q_START <= s0 + ds < Q_END:
                mask[:, ds] = 1.0
        in_maps.append(dict(
            xin=x_c, wallpA=wallpA, wallpB=wallpB,
            wqbT=per_core_w[c][0], wkvnT=per_core_w[c][1], wkvvT=per_core_w[c][2],
            bvec=bvec, lnw=lnw, rb=rb, consts=consts,
            CS=CS2.astype(ml_dtypes.bfloat16),
            CSL=np.ascontiguousarray(CS2[:, s0:s0 + SCH]).astype(ml_dtypes.bfloat16),
            identf=identf, trib=trib, expf=expf, ones1f=ones1f, ones3f=ones3f,
            onesb=onesb, onesf=onesf, maskb=mask.astype(ml_dtypes.bfloat16)))
    return in_maps, offs, tuple(wps)



class Bld:
    ENGS = ("sync", "gpsimd", "tensor", "scalar", "vector")

    def __init__(self, nc):
        self.nc = nc
        self.ops = {e: [] for e in self.ENGS}
        self.plan = {}
        self.sems = {}

    def emit(self, eng, fn):
        self.ops[eng].append(fn)

    def inc(self, eng, build_fn, sname, k=1):
        self.plan[sname] = self.plan.get(sname, 0) + k
        self.emit(eng, lambda e, _b=build_fn, _s=sname, _k=k: _b(e).then_inc(self.sems[_s], _k))
        return self.plan[sname]

    def dma(self, eng, out, in_, sname):
        return self.inc(eng, lambda e, o=out, i=in_: e.dma_start(out=o, in_=i), sname, 16)

    def wait(self, eng, sname, val=None):
        v = self.plan.get(sname, 0) if val is None else val
        if v > 0:
            self.emit(eng, lambda e, _s=sname, _v=v: e.wait_ge(self.sems[_s], _v))

    def mark(self, sname):
        return self.plan.get(sname, 0)


class Pp:
    def __init__(self, bld, banks):
        self.bld, self.banks = bld, banks
        self.ev = [None] * len(banks)

    def acq(self, i):
        if self.ev[i] is not None:
            self.bld.wait("tensor", self.ev[i][0], self.ev[i][1])
            self.ev[i] = None
        return self.banks[i]

    def rel(self, i, sem):
        self.ev[i] = (sem, self.bld.mark(sem))



def build_program(offs, wps, stage):
    nc = bass.Bass("TRN2", target_bir_lowering=False, debug=False, detect_race_conditions=False)

    def din(name, shape, dt):
        return nc.dram_tensor(name, shape, dt, kind="ExternalInput").ap()

    xin = din("xin", [D, TLOC], F32)
    wallpA = din("wallpA", [128, wps[0]], F32)
    wallpB = din("wallpB", [128, wps[1]], F32)
    wqbT_h = din("wqbT", [QLAT, 512], F32)
    wkvnT_h = din("wkvnT", [KVR, 256], F32)
    wkvvT_h = din("wkvvT", [KVR, 512], F32)
    bvec_h = din("bvec", [128, 64], F32)
    lnw_h = din("lnw", [128, 16], F32)
    rb_h = din("rb", [8, 1], F32)
    consts_h = din("consts", [128, NCONST], F32)
    CS_h = din("CS", [128, S], BF16)
    CSL_h = din("CSL", [128, SCH], BF16)
    identf_h = din("identf", [128, 128], F32)
    trib_h = din("trib", [128, 128], BF16)
    expf_h = din("expf", [8, 128], F32)
    ones1f_h = din("ones1f", [1, 128], F32)
    ones3f_h = din("ones3f", [96, 128], F32)
    onesb_h = din("onesb", [128, 1], BF16)
    onesf_h = din("onesf", [128, 1], F32)
    maskb_h = din("maskb", [128, SCH], BF16)

    xout = nc.dram_tensor("xout", [D, TLOC], F32, kind="ExternalOutput").ap()

    wall_inA = nc.dram_tensor("wall_inA", [128, wps[0]], BF16)
    wall_gA = nc.dram_tensor("wall_gA", [NCORES * 128, wps[0]], BF16, addr_space="Shared")
    wall_inB = nc.dram_tensor("wall_inB", [128, wps[1]], BF16)
    wall_gB = nc.dram_tensor("wall_gB", [NCORES * 128, wps[1]], BF16, addr_space="Shared")
    sc_in = nc.dram_tensor("sc_in", [128, 320], F32)
    sc_out = nc.dram_tensor("sc_out", [128, 320], F32)
    lat_in = nc.dram_tensor("lat_in", [LATR, TLOC], BF16)
    lat_g = nc.dram_tensor("lat_g", [NCORES * LATR, TLOC], BF16, addr_space="Shared")
    at_in = nc.dram_tensor("at_in", [NCORES * HLOC * VD, TLOC], BF16)
    at_a2a = nc.dram_tensor("at_a2a", [NCORES * HLOC * VD, TLOC], BF16)
    rt_in = nc.dram_tensor("rt_in", [128, 128], F32)
    rt_out = nc.dram_tensor("rt_out", [128, 128], F32)

    wall_flats = [wall_gA.ap().rearrange("a b -> (a b)"),
                  wall_gB.ap().rearrange("a b -> (a b)")]

    def wv(name, r0, r1, c0, c1):
        wid, off, shp = offs[name]
        return (wall_flats[wid][off:off + shp[0] * shp[1]]
                .rearrange("(a b) -> a b", b=shp[1])[r0:r1, c0:c1])

    lat_v = lat_g.ap().rearrange("(r l) t -> r l t", l=LATR)
    atin_v = at_in.ap().rearrange("(d r) t -> d r t", r=HLOC * VD)

    a = nc.alloc_sbuf_tensor
    x_t = a("sb_x", [128, NT * TLOC], F32)
    big = a("sb_big", [128, 16384], BF16)
    bigf = big.bitcast(F32)
    s2 = a("sb_s2", [128, 5120], BF16)
    RS = a("sb_RS", [128, 512], F32)
    MRS = a("sb_MRS", [128, 512], F32)
    stats = a("sb_stats", [128, 512], F32)
    wqb = a("sb_wqb", [128, 4096], BF16)
    wkvn = a("sb_wkvn", [128, 512], BF16)
    wkvv = a("sb_wkvv", [128, 1024], BF16)
    dbw = a("sb_dbw", [128, 1024], BF16)
    dba = a("sb_dba", [128, 256], BF16)
    sqs = a("sb_sqs", [128, 1024], BF16)
    Pt = a("sb_P", [128, 1024], BF16)
    scrA = a("sb_scrA", [128, 1024], BF16)
    scrB = a("sb_scrB", [128, 256], F32)
    pe_scr = a("sb_pe", [128, 512], BF16)
    exch = a("sb_exch", [128, 1024], BF16)
    rpart = a("sb_rpart", [128, 128], F32)
    CS2 = a("sb_CS2", [128, S], BF16)
    CSL = a("sb_CSL", [128, SCH], BF16)
    maskb = a("sb_maskb", [128, SCH], BF16)
    identf = a("sb_identf", [128, 128], F32)
    trib = a("sb_trib", [128, 128], BF16)
    expf = a("sb_expf", [8, 128], F32)
    ones1f = a("sb_ones1f", [1, 128], F32)
    ones3f = a("sb_ones3f", [96, 128], F32)
    onesb = a("sb_onesb", [128, 1], BF16)
    onesf = a("sb_onesf", [128, 1], F32)
    cst = a("sb_cst", [128, NCONST], F32)
    bvec = a("sb_bvec", [128, 64], F32)
    lnw = a("sb_lnw", [128, 16], F32)
    rbt = a("sb_rb", [8, 1], F32)
    assert nc.sbuf_bytes_remaining >= 0, nc.sbuf_bytes_remaining

    SFULL = bigf[:, 1024:1536]
    SACC = bigf[:, 1536:1856]
    RS2 = bigf[:, 2048:2560]
    MRS2 = bigf[:, 2560:3072]

    QA = 0
    KF = 0
    QF = 4096
    VT = 8192
    CKV = 12288

    ps = [nc.alloc_psum_tensor(f"ps{i}", [128, 512], F32) for i in range(8)]
    bld = Bld(nc)
    pp = Pp(bld, ps)
    rg = [list(range(NCORES))]

    bld.dma("sync", x_t[:, :].rearrange("p (t c) -> p t c", t=NT),
            xin.rearrange("(t p) c -> p t c", p=128), "s_x")
    for dst, src in [(cst, consts_h), (bvec, bvec_h), (lnw, lnw_h), (rbt, rb_h),
                     (CS2, CS_h), (CSL, CSL_h), (identf, identf_h), (trib, trib_h),
                     (expf, expf_h), (ones1f, ones1f_h), (ones3f, ones3f_h),
                     (onesb, onesb_h), (onesf, onesf_h), (maskb, maskb_h)]:
        bld.dma("sync", dst[:, :], src, "s_x")
    bld.dma("gpsimd", wqb[:, :].rearrange("p (k m) -> p k m", k=8),
            wqbT_h.rearrange("(k p) m -> p k m", p=128), "s_k")
    bld.dma("gpsimd", wkvn[:, :].rearrange("p (k m) -> p k m", k=2),
            wkvnT_h.rearrange("(k p) m -> p k m", p=128), "s_k")
    bld.dma("gpsimd", wkvv[:, :].rearrange("p (k m) -> p k m", k=2),
            wkvvT_h.rearrange("(k p) m -> p k m", p=128), "s_k")

    slot_ev = [0, 0]
    i = 0
    wall_marks = []
    for wall_src, wall_dst, wp in ((wallpA, wall_inA, wps[0]),
                                   (wallpB, wall_inB, wps[1])):
        for c0 in range(0, wp, 2048):
            c1 = min(c0 + 2048, wp)
            sl_ = (i % 2) * 2048
            gsem = "s_g0" if i % 2 == 0 else "s_g1"
            if i >= 2:
                bld.wait("gpsimd", gsem, slot_ev[i % 2])
            bld.dma("gpsimd", s2[:, sl_:sl_ + (c1 - c0)], wall_src[:, c0:c1], "s_k2")
            bld.wait("gpsimd", "s_k2", bld.mark("s_k2"))
            slot_ev[i % 2] = bld.dma("gpsimd", wall_dst.ap()[:, c0:c1],
                                     s2[:, sl_:sl_ + (c1 - c0)], gsem)
            i += 1
        wall_marks.append((bld.mark("s_g0"), bld.mark("s_g1")))

    for eng in ("tensor", "scalar", "vector"):
        bld.wait(eng, "s_x")
        bld.wait(eng, "s_k", 48)

    def vops(*fns):
        for fn in fns:
            bld.inc("vector", fn, "s_ve")
            bld.wait("vector", "s_ve")
    sq_ev = [0, 0]

    def ln_stats(so, rs_full, rs_row, mrs_full, mrs_row):
        bld.wait("scalar", "s_ve")
        bld.wait("tensor", "s_ve")
        pm, pq = pp.acq(0), pp.acq(1)
        for k in range(NT):
            xs = x_t[:, k * TLOC + so: k * TLOC + so + 512]
            sl_ = (k % 2) * 512
            if k >= 2:
                bld.wait("scalar", "s_mm", sq_ev[k % 2])
            bld.inc("scalar", lambda e, o=sqs[:, sl_:sl_ + 512], i=xs:
                    e.activation(o, i, AF.Square), "s_sc")
            bld.wait("tensor", "s_sc")
            bld.inc("tensor", lambda e, o=pm[0:1, :], i=xs, k_=k:
                    e.matmul(o, onesf[:, 0:1], i, start=(k_ == 0), stop=(k_ == NT - 1)), "s_mm")
            sq_ev[k % 2] = bld.inc("tensor",
                    lambda e, o=pq[0:1, :], i=sqs[:, sl_:sl_ + 512], k_=k:
                    e.matmul(o, onesb[:, 0:1], i, start=(k_ == 0), stop=(k_ == NT - 1)), "s_mm")
        bld.wait("vector", "s_mm")
        A = stats[0:1, 0:512]
        bld.inc("vector", lambda e: e.tensor_scalar(A, pm[0:1, :], cst[0:1, C_I4096:C_I4096 + 1], None, ALU.mult), "s_ve")
        bld.inc("vector", lambda e: e.tensor_scalar(mrs_row, pq[0:1, :], cst[0:1, C_I4096:C_I4096 + 1], None, ALU.mult), "s_ve")
        pp.rel(0, "s_ve"); pp.rel(1, "s_ve")
        bld.inc("vector", lambda e: e.tensor_mul(rs_row, A, A), "s_ve")
        bld.inc("vector", lambda e: e.tensor_sub(mrs_row, mrs_row, rs_row), "s_ve")
        bld.wait("scalar", "s_ve")
        bld.inc("scalar", lambda e: e.activation(rs_row, mrs_row, AF.Sqrt, bias=cst[0:1, C_EPS5:C_EPS5 + 1]), "s_sc")
        bld.wait("vector", "s_sc")
        bld.inc("vector", lambda e: e.reciprocal(rs_row, rs_row), "s_ve")
        bld.inc("vector", lambda e: e.tensor_mul(mrs_row, A, rs_row), "s_ve")
        bcast(rs_row, rs_full, 2)
        bcast(mrs_row, mrs_full, 3)
        bld.wait("vector", "s_sc")

    def bcast(row, full, bank, lhs=None):
        pb = pp.acq(bank)
        bld.wait("tensor", "s_ve")
        L = ones1f[0:1, :] if lhs is None else lhs
        bld.inc("tensor", lambda e, r=row, L=L: e.matmul(pb[:, :], L, r, start=True, stop=True), "s_mm")
        bld.wait("scalar", "s_mm")
        bld.inc("scalar", lambda e, f=full: e.copy(f, pb[:, :]), "s_sc")
        pp.rel(bank, "s_sc")
        bld.wait("vector", "s_sc")

    def xn_gen(k, so, dst, rs_full, mrs_full):
        xs = x_t[:, k * TLOC + so: k * TLOC + so + 512]
        o = big[:, dst:dst + 512]
        bld.inc("vector", lambda e, o=o, xs=xs, r=rs_full: e.tensor_mul(o, xs, r), "s_ve")
        bld.inc("vector", lambda e, o=o, m=mrs_full: e.tensor_sub(o, o, m), "s_ve")

    def stream2(groups, consume, buf, slotw):
        n = len(groups)
        ldv = [0] * n
        ent = bld.mark("s_mm")
        used = [ent, ent]
        for g in range(n):
            sl_ = g % 2
            q = "sync" if sl_ == 0 else "gpsimd"
            bld.wait(q, "s_mm", used[sl_])
            ap_, w_ = groups[g]
            ldv[g] = bld.dma(q, buf[:, sl_ * slotw: sl_ * slotw + w_]
                             .rearrange("p (g w) -> p g w", w=ap_.shape[-1]), ap_,
                             "s_w0" if sl_ == 0 else "s_w1")
            if g >= 1:
                kk = g - 1
                bld.wait("tensor", "s_w0" if kk % 2 == 0 else "s_w1", ldv[kk])
                used[kk % 2] = consume(kk, (kk % 2) * slotw)
        kk = n - 1
        bld.wait("tensor", "s_w0" if kk % 2 == 0 else "s_w1", ldv[kk])
        used[kk % 2] = consume(kk, (kk % 2) * slotw)

    def gview(name, k0, ntile, c0, c1):
        return (wv(name, k0 * 128, (k0 + ntile) * 128, c0, c1)
                .rearrange("(g p) w -> p g w", p=128))

    def wchain_full(wname, bname, rhs_last, v_hw, epi):
        groups, meta = [], []
        for mg in range(8):
            for g8 in range(8):
                groups.append((gview(wname, g8 * 4, 4, mg * 512, (mg + 1) * 512), 2048))
                meta.append((mg, g8 * 4))
            if bname is not None:
                groups.append((gview(bname, 0, 1, mg * 512, (mg + 1) * 512), 512))
                meta.append((mg, -1))
        pbs = {}

        def consume(g, col):
            mg, k0 = meta[g]
            if k0 == 0:
                bs = 4 if mg % 2 == 0 else 0
                pbs[mg] = [pp.acq(bs + j) for j in range(4)]
            pb_ = pbs[mg]
            last = 0
            if k0 >= 0:
                for t in range(4):
                    k = k0 + t
                    stop = (bname is None and k == NT - 1)
                    for j in range(4):
                        last = bld.inc("tensor", lambda e, c=col + t * 512, j_=j,
                                       k_=k, pb=pb_, st=stop: e.matmul(
                            pb[j_][:, :], s2[:, c + j_ * 128:c + (j_ + 1) * 128],
                            big[:, k_ * 512:(k_ + 1) * 512],
                            start=(k_ == 0), stop=st), "s_mm")
            else:
                bld.wait("tensor", "s_ve", v_hw)
                for j in range(4):
                    last = bld.inc("tensor", lambda e, c=col, j_=j, pb=pb_: e.matmul(
                        pb[j_][:, :], s2[:, c + j_ * 128:c + (j_ + 1) * 128],
                        rhs_last, start=False, stop=True), "s_mm")
            if k0 == -1 or (bname is None and k0 == 28):
                bld.wait("vector", "s_mm")
                epi(mg, pb_)
                bs = 4 if mg % 2 == 0 else 0
                for j in range(4):
                    pp.rel(bs + j, "s_ve")
            return last
        stream2(groups, consume, s2, 2048)

    def finish():
        bld.wait("sync", "s_ve")
        bld.wait("sync", "s_sc")
        bld.wait("sync", "s_mm")
        bld.dma("sync", xout.rearrange("(t p) c -> p t c", p=128),
                x_t[:, :].rearrange("p (t c) -> p t c", t=NT), "s_st")
        bld.wait("sync", "s_st")
        return _finalize(nc, bld)

    ln_stats(0, RS[:, 0:512], RS[0:1, 0:512], MRS[:, 0:512], MRS[0:1, 0:512])
    ln_stats(512, RS2, RS2[0:1, :], MRS2, MRS2[0:1, :])

    PAIRS = [(n, m) for n in range(B) for m in range(n, B)]
    pidx = {p: i for i, p in enumerate(PAIRS)}

    xn_ev = [0, 0]
    for k in range(NT):
        dst = (k % 2) * 1024
        if k >= 2:
            bld.wait("vector", "s_ve", xn_ev[k % 2])
        xn_gen(k, 0, dst, RS[:, 0:512], MRS[:, 0:512])
        xn_gen(k, 512, dst + 512, RS2, MRS2)
        for (n, m) in PAIRS:
            col = k * 10 + pidx[(n, m)]
            bld.inc("vector", lambda e, n_=n, m_=m, d=dst, c=col:
                    e.scalar_tensor_tensor(
                        out=sqs[:, 0:256], in0=big[:, d + n_ * 256: d + n_ * 256 + 256],
                        scalar=1.0, in1=big[:, d + m_ * 256: d + m_ * 256 + 256],
                        op0=ALU.mult, op1=ALU.mult, accum_out=SACC[:, c:c + 1]), "s_ve")
        xn_ev[k % 2] = bld.mark("s_ve")

    bld.wait("sync", "s_ve")
    bld.dma("sync", sc_in.ap(), SACC, "s_gc")
    bld.wait("gpsimd", "s_gc")
    bld.inc("gpsimd", lambda e: e.collective_compute(
        "AllReduce", ALU.add, replica_groups=rg,
        ins=[sc_in.ap()], outs=[sc_out.ap()]), "s_cc", 1)
    CC_SCORE = bld.mark("s_cc")
    bld.wait("gpsimd", "s_g0", wall_marks[0][0])
    bld.wait("gpsimd", "s_g1", wall_marks[0][1])
    bld.inc("gpsimd", lambda e: e.collective_compute(
        "AllGather", ALU.bypass, replica_groups=rg,
        ins=[wall_inA.ap()], outs=[wall_gA.ap()]), "s_cc", 1)
    CC_WALL = bld.mark("s_cc")
    bld.wait("gpsimd", "s_g0", wall_marks[1][0])
    bld.wait("gpsimd", "s_g1", wall_marks[1][1])
    bld.inc("gpsimd", lambda e: e.collective_compute(
        "AllGather", ALU.bypass, replica_groups=rg,
        ins=[wall_inB.ap()], outs=[wall_gB.ap()]), "s_cc", 1)
    CC_WALLB = bld.mark("s_cc")
    bld.wait("sync", "s_cc", CC_SCORE)
    bld.dma("sync", SACC, sc_out.ap(), "s_gc")
    bld.wait("vector", "s_gc")

    for n in range(B):
        for m in range(B):
            pi = pidx[(n, m) if n <= m else (m, n)]
            bld.inc("vector", lambda e, n_=n, m_=m, p_=pi: e.tensor_copy(
                SFULL[:, n_ * 4 + m_::16], SACC[:, p_::10]), "s_ve")
    M1 = scrB[:, 0:128]
    S1 = scrB[:, 128:256]
    bld.inc("vector", lambda e: e.tensor_max(M1, SFULL[:, 0::4], SFULL[:, 1::4]), "s_ve")
    bld.inc("vector", lambda e: e.tensor_max(M1, M1, SFULL[:, 2::4]), "s_ve")
    bld.inc("vector", lambda e: e.tensor_max(M1, M1, SFULL[:, 3::4]), "s_ve")
    for m in range(B):
        bld.inc("vector", lambda e, m_=m: e.tensor_sub(SFULL[:, m_::4], SFULL[:, m_::4], M1), "s_ve")
    bld.wait("scalar", "s_ve")
    bld.inc("scalar", lambda e: e.activation(SFULL[:, :], SFULL[:, :], AF.Exp,
            scale=cst[:, C_L0SC:C_L0SC + 1]), "s_sc")
    bld.wait("vector", "s_sc")
    bld.inc("vector", lambda e: e.tensor_add(S1, SFULL[:, 0::4], SFULL[:, 1::4]), "s_ve")
    bld.inc("vector", lambda e: e.tensor_add(S1, S1, SFULL[:, 2::4]), "s_ve")
    bld.inc("vector", lambda e: e.tensor_add(S1, S1, SFULL[:, 3::4]), "s_ve")
    bld.inc("vector", lambda e: e.reciprocal(S1, S1), "s_ve")
    for m in range(B):
        bld.inc("vector", lambda e, m_=m: e.tensor_mul(SFULL[:, m_::4], SFULL[:, m_::4], S1), "s_ve")
    bld.wait("vector", "s_ve")
    for n in range(B):
        bld.inc("vector", lambda e, n_=n: e.tensor_scalar(
            SFULL[:, n_ * 5::16], SFULL[:, n_ * 5::16], 1.0, None, ALU.add), "s_ve")
    bld.wait("vector", "s_ve")

    for k in range(NT):
        dst = (k % 2) * 1024
        bld.wait("vector", "s_ve", xn_ev[k % 2])
        xn_gen(k, 0, dst, RS[:, 0:512], MRS[:, 0:512])
        xn_gen(k, 512, dst + 512, RS2, MRS2)
        for n in range(B):
            for m in range(B):
                col = k * 16 + n * 4 + m
                bld.inc("vector", lambda e, n_=n, m_=m, d=dst, c=col, k_=k:
                        e.scalar_tensor_tensor(
                            out=x_t[:, k_ * TLOC + n_ * 256: k_ * TLOC + n_ * 256 + 256],
                            in0=big[:, d + m_ * 256: d + m_ * 256 + 256],
                            scalar=SFULL[:, c:c + 1],
                            in1=x_t[:, k_ * TLOC + n_ * 256: k_ * TLOC + n_ * 256 + 256],
                            op0=ALU.mult, op1=ALU.add), "s_ve")
        xn_ev[k % 2] = bld.mark("s_ve")

    if stage == 1:
        return nc, finish()

    bld.wait("sync", "s_cc", CC_WALL)
    bld.wait("gpsimd", "s_cc", CC_WALL)
    bld.wait("sync", "s_g0", wall_marks[1][0])
    bld.wait("sync", "s_g1", wall_marks[1][1])

    def moe_ffn(layer):
        pass

    for sl in range(2):
        so = sl * 512
        ln_stats(so, RS[:, 0:512], RS[0:1, 0:512], MRS[:, 0:512], MRS[0:1, 0:512])
        for k in range(NT):
            xn_gen(k, so, k * 512, RS[:, 0:512], MRS[:, 0:512])
        V_XN = bld.mark("s_ve")

        bld.dma("sync", dba[:, :].rearrange("p (k e) -> p k e", k=32),
                wv("G0T", 0, D, 0, 8).rearrange("(k p) e -> p k e", p=128), "s_wS")
        pg = pp.acq(1)
        bld.wait("tensor", "s_wS")
        bld.wait("tensor", "s_ve", V_XN)
        for k in range(NT):
            bld.inc("tensor", lambda e, k_=k: e.matmul(
                pg[0:8, :], dba[:, k_ * 8:(k_ + 1) * 8], big[:, k_ * 512:(k_ + 1) * 512],
                start=(k_ == 0), stop=(k_ == NT - 1)), "s_mm")
        DBA_FREE = bld.mark("s_mm")
        bld.wait("scalar", "s_mm")
        bld.inc("scalar", lambda e: e.copy(stats[0:8, 0:512], pg[0:8, :]), "s_sc")
        pp.rel(1, "s_sc")
        pt = pp.acq(2)
        bld.wait("tensor", "s_sc")
        for q in range(4):
            bld.inc("tensor", lambda e, q_=q: e.transpose(
                pt[:, q_ * 8:(q_ + 1) * 8], stats[0:8, q_ * 128:(q_ + 1) * 128],
                identf[0:8, 0:8]), "s_mm")
        bld.wait("scalar", "s_mm")
        bld.inc("scalar", lambda e: e.copy(scrB[:, 0:32], pt[:, 0:32]), "s_sc")
        pp.rel(2, "s_sc")
        bld.wait("vector", "s_sc")
        for q in range(4):
            lt = scrB[:, q * 8:(q + 1) * 8]
            m1 = scrB[:, 32 + q:33 + q]
            m1n = scrB[:, 36 + q:37 + q]
            m2 = scrB[:, 40 + q:41 + q]
            ms = scrB[:, 44:52]
            ex = scrB[:, 52:60]
            sm = scrB[:, 60:61]
            vops(lambda e, lt=lt, m1=m1: e.reduce_max(m1, lt, AX),
                 lambda e, m1=m1, m1n=m1n: e.tensor_scalar(m1n, m1, -1.0, None, ALU.mult),
                 lambda e, lt=lt, m1=m1, ms=ms: e.tensor_single_scalar(ms, lt, m1, ALU.is_ge),
                 lambda e, lt=lt, ms=ms, ex=ex: e.scalar_tensor_tensor(
                     out=ex, in0=ms, scalar=-1e30, in1=lt, op0=ALU.mult, op1=ALU.add),
                 lambda e, ex=ex, m2=m2: e.reduce_max(m2, ex, AX),
                 lambda e, lt=lt, m2=m2, ms=ms: e.tensor_single_scalar(ms, lt, m2, ALU.is_ge))
            bld.wait("scalar", "s_ve")
            bld.inc("scalar", lambda e, lt=lt, ex=ex, m1n=m1n: e.activation(ex, lt, AF.Exp, bias=m1n), "s_sc")
            bld.wait("scalar", "s_sc")
            bld.wait("vector", "s_sc")
            vops(lambda e, ex=ex, ms=ms: e.tensor_mul(ex, ex, ms),
                 lambda e, ex=ex, sm=sm: e.reduce_sum(sm, ex, AX),
                 lambda e, sm=sm: e.reciprocal(sm, sm),
                 lambda e, ex=ex, sm=sm: e.tensor_scalar(ex, ex, sm, None, ALU.mult))
            pt2 = pp.acq(2)
            bld.wait("tensor", "s_ve")
            bld.inc("tensor", lambda e, q_=q, ex=ex: e.transpose(
                pt2[0:8, q_ * 128:(q_ + 1) * 128], ex, identf[:, :]), "s_mm")
            bld.wait("scalar", "s_mm")
            bld.inc("scalar", lambda e, q_=q: e.copy(
                stats[0:8, q_ * 128:(q_ + 1) * 128], pt2[0:8, q_ * 128:(q_ + 1) * 128]), "s_sc")
            pp.rel(2, "s_sc")
            bld.wait("vector", "s_mm")
        pe_ = pp.acq(3)
        bld.wait("tensor", "s_sc")
        bld.inc("tensor", lambda e: e.matmul(pe_[:, :], expf[:, :], stats[0:8, 0:512],
                start=True, stop=True), "s_mm")
        bld.wait("scalar", "s_mm")
        bld.inc("scalar", lambda e: e.copy(scrA[:, 512:1024], pe_[:, :]), "s_sc")
        pp.rel(3, "s_sc")

        ph = pp.acq(0)
        groups = [(gview("A0T", g4 * 4, 4, 0, 128), 512) for g4 in range(8)]

        def use_a0(g, col):
            last = 0
            for t in range(4):
                k = g * 4 + t
                last = bld.inc("tensor", lambda e, k_=k, c=col + t * 128: e.matmul(
                    ph[:, :], dbw[:, c:c + 128], big[:, k_ * 512:(k_ + 1) * 512],
                    start=(k_ == 0), stop=(k_ == NT - 1)), "s_mm")
            return last
        stream2(groups, use_a0, dbw, 512)
        bld.wait("vector", "s_mm")
        bld.wait("vector", "s_sc")
        bld.inc("vector", lambda e: e.tensor_mul(scrA[:, 0:512], ph[:, :], scrA[:, 512:1024]), "s_ve")
        pp.rel(0, "s_ve")
        V_HW = bld.mark("s_ve")
        if DBG == "l0g" and sl == 0:
            bld.wait("sync", "s_ve")
            bld.wait("sync", "s_sc")
            scrAf = scrA.bitcast(F32)
            bld.dma("sync", xout[0:128, 0:512], scrAf[:, 0:512], "s_st")
            bld.dma("sync", xout[128:136, 0:512], stats[0:8, 0:512], "s_st")
            bld.dma("sync", xout[136:264, 0:64], scrB[:, 0:64], "s_st")
            bld.dma("sync", xout[264:392, 0:256], big.bitcast(F32)[:, 0:256], "s_st")
            bld.wait("sync", "s_st")
            return nc, _finalize(nc, bld)

        def epi_w0(mg, pb_, so_=so):
            for j in range(4):
                mt = mg * 4 + j
                bld.inc("vector", lambda e, j_=j, mt_=mt, pb=pb_: e.scalar_tensor_tensor(
                    out=x_t[:, mt_ * TLOC + so_: mt_ * TLOC + so_ + 512],
                    in0=pb[j_][:, :], scalar=bvec[:, mt_:mt_ + 1],
                    in1=x_t[:, mt_ * TLOC + so_: mt_ * TLOC + so_ + 512],
                    op0=ALU.add, op1=ALU.add), "s_ve")
        wchain_full("W0T", "B0S", scrA[:, 0:512], V_HW, epi_w0)

    if stage == 2:
        return nc, finish()

    bld.wait("sync", "s_cc", CC_WALLB)
    bld.wait("gpsimd", "s_cc", CC_WALLB)
    for sl in range(2):
        so = sl * 512
        ln_stats(so, RS[:, 0:512], RS[0:1, 0:512], MRS[:, 0:512], MRS[0:1, 0:512])
        for k in range(NT):
            xn_gen(k, so, k * 512, RS[:, 0:512], MRS[:, 0:512])
        V_XN = bld.mark("s_ve")
        bld.wait("tensor", "s_ve", V_XN)

        pss = pp.acq(2)
        qa_groups = [(gview("QAT", g4 * 4, 4, m * 128, (m + 1) * 128), 512)
                     for m in range(8) for g4 in range(8)]
        pqs = {}

        def use_qa(g, col):
            m, g8 = divmod(g, 8)
            if g8 == 0:
                pqs[m] = pp.acq(m % 2)
            pq_ = pqs[m]
            last = 0
            for t in range(4):
                k = g8 * 4 + t
                last = bld.inc("tensor", lambda e, k_=k, c=col + t * 128, pq=pq_: e.matmul(
                    pq[:, :], dbw[:, c:c + 128], big[:, k_ * 512:(k_ + 1) * 512],
                    start=(k_ == 0), stop=(k_ == NT - 1)), "s_mm")
            if g8 == 7:
                bld.wait("scalar", "s_mm")
                bld.inc("scalar", lambda e, m_=m, pq=pq_: e.copy(
                    s2[:, m_ * 512:(m_ + 1) * 512], pq[:, :]), "s_sc")
                bld.inc("scalar", lambda e, m_=m, pq=pq_: e.activation(
                    sqs[:, (m_ % 2) * 512:(m_ % 2) * 512 + 512], pq[:, :], AF.Square), "s_sc")
                bld.wait("tensor", "s_sc")
                last = bld.inc("tensor", lambda e, m_=m: e.matmul(
                    pss[0:1, :], onesb[:, 0:1], sqs[:, (m_ % 2) * 512:(m_ % 2) * 512 + 512],
                    start=(m_ == 0), stop=(m_ == 7)), "s_mm")
                pp.rel(m % 2, "s_sc")
            return last
        stream2(qa_groups, use_qa, dbw, 512)
        bld.wait("vector", "s_mm")
        A = stats[0:1, 0:512]
        bld.inc("vector", lambda e: e.tensor_scalar(A, pss[0:1, :], cst[0:1, C_I1024:C_I1024 + 1], None, ALU.mult), "s_ve")
        pp.rel(2, "s_ve")
        bld.wait("scalar", "s_ve")
        bld.inc("scalar", lambda e: e.activation(RS[0:1, 0:512], A, AF.Sqrt, bias=cst[0:1, C_EPS6:C_EPS6 + 1]), "s_sc")
        bld.wait("tensor", "s_sc")
        bcast(RS[0:1, 0:512], RS[:, 0:512], 3)
        bld.inc("vector", lambda e: e.reciprocal(RS[:, 0:512], RS[:, 0:512]), "s_ve")
        bld.wait("vector", "s_sc")
        ex_ev = [0, 0]
        for m in range(8):
            sl2 = (m % 2) * 512
            gs = "s_g0" if m % 2 == 0 else "s_g1"
            if m >= 2:
                bld.wait("vector", gs, ex_ev[m % 2])
            bld.inc("vector", lambda e, m_=m, s_=sl2: e.scalar_tensor_tensor(
                out=exch[:, s_:s_ + 512], in0=s2[:, m_ * 512:(m_ + 1) * 512],
                scalar=lnw[:, m_:m_ + 1], in1=RS[:, 0:512],
                op0=ALU.mult, op1=ALU.mult), "s_ve")
            bld.wait("gpsimd", "s_ve")
            ex_ev[m % 2] = bld.dma("gpsimd",
                    lat_in.ap()[m * 128:(m + 1) * 128, so:so + 512],
                    exch[:, sl2:sl2 + 512], gs)

        pss2 = pp.acq(2)
        KVM = [(128, 0), (128, 128), (64, 256)]
        kv_groups = [(gview("KAT", g4 * 4, 4, mof, mof + mw), 4 * mw)
                     for (mw, mof) in KVM for g4 in range(8)]
        pqs2 = {}

        def use_kv(g, col):
            mi, g8 = divmod(g, 8)
            mw = KVM[mi][0]
            if g8 == 0:
                pqs2[mi] = pp.acq(mi % 2)
            pq_ = pqs2[mi]
            last = 0
            for t in range(4):
                k = g8 * 4 + t
                last = bld.inc("tensor", lambda e, k_=k, c=col + t * mw, w_=mw, pq=pq_: e.matmul(
                    pq[0:w_, :], dbw[:, c:c + w_], big[:, k_ * 512:(k_ + 1) * 512],
                    start=(k_ == 0), stop=(k_ == NT - 1)), "s_mm")
            if g8 == 7:
                bld.wait("scalar", "s_mm")
                if mi < 2:
                    bld.inc("scalar", lambda e, m_=mi, pq=pq_: e.copy(
                        s2[:, 4096 + m_ * 512: 4096 + (m_ + 1) * 512], pq[:, :]), "s_sc")
                    bld.inc("scalar", lambda e, m_=mi, pq=pq_: e.activation(
                        sqs[:, (m_ % 2) * 512:(m_ % 2) * 512 + 512], pq[:, :], AF.Square), "s_sc")
                    bld.wait("tensor", "s_sc")
                    last = bld.inc("tensor", lambda e, m_=mi: e.matmul(
                        pss2[0:1, :], onesb[:, 0:1], sqs[:, (m_ % 2) * 512:(m_ % 2) * 512 + 512],
                        start=(m_ == 0), stop=(m_ == 1)), "s_mm")
                else:
                    bld.inc("scalar", lambda e, pq=pq_: e.copy(pe_scr[0:64, :], pq[0:64, :]), "s_sc")
                    bld.inc("scalar", lambda e, pq=pq_: e.copy(pe_scr[64:96, :], pq[32:64, :]), "s_sc")
                    bld.inc("scalar", lambda e, pq=pq_: e.copy(pe_scr[96:128, :], pq[0:32, :]), "s_sc")
                pp.rel(mi % 2, "s_sc")
            return last
        stream2(kv_groups, use_kv, dbw, 512)
        bld.wait("vector", "s_mm")
        bld.inc("vector", lambda e: e.tensor_scalar(A, pss2[0:1, :], cst[0:1, C_I256:C_I256 + 1], None, ALU.mult), "s_ve")
        pp.rel(2, "s_ve")
        bld.wait("scalar", "s_ve")
        bld.inc("scalar", lambda e: e.activation(MRS[0:1, 0:512], A, AF.Sqrt, bias=cst[0:1, C_EPS6:C_EPS6 + 1]), "s_sc")
        bld.wait("tensor", "s_sc")
        bcast(MRS[0:1, 0:512], MRS[:, 0:512], 3)
        bld.inc("vector", lambda e: e.reciprocal(MRS[:, 0:512], MRS[:, 0:512]), "s_ve")
        bld.wait("vector", "s_sc")
        for m in range(2):
            sl2 = (m % 2) * 512
            gs = "s_g0" if m % 2 == 0 else "s_g1"
            bld.wait("vector", gs, ex_ev[m % 2])
            bld.inc("vector", lambda e, m_=m, s_=sl2: e.scalar_tensor_tensor(
                out=exch[:, s_:s_ + 512], in0=s2[:, 4096 + m_ * 512: 4096 + (m_ + 1) * 512],
                scalar=lnw[:, 8 + m_:9 + m_], in1=MRS[:, 0:512],
                op0=ALU.mult, op1=ALU.mult), "s_ve")
            bld.wait("gpsimd", "s_ve")
            ex_ev[m % 2] = bld.dma("gpsimd",
                    lat_in.ap()[QLAT + m * 128:QLAT + (m + 1) * 128, so:so + 512],
                    exch[:, sl2:sl2 + 512], gs)
        bld.wait("vector", "s_sc")
        bld.wait("vector", "s_g0", ex_ev[0])
        bld.wait("vector", "s_g1", ex_ev[1])
        for bb in range(2):
            c0 = bb * 256
            bld.inc("vector", lambda e, c0=c0: e.tensor_mul(
                scrA[0:64, 0:256], pe_scr[0:64, c0:c0 + 256], CSL[0:64, :]), "s_ve")
            bld.inc("vector", lambda e, c0=c0: e.tensor_mul(
                scrA[0:64, 256:512], pe_scr[64:128, c0:c0 + 256], CSL[64:128, :]), "s_ve")
            bld.inc("vector", lambda e, c0=c0: e.tensor_sub(
                exch[0:32, c0:c0 + 256], scrA[0:32, 0:256], scrA[0:32, 256:512]), "s_ve")
            bld.inc("vector", lambda e, c0=c0: e.tensor_add(
                exch[32:64, c0:c0 + 256], scrA[32:64, 0:256], scrA[32:64, 256:512]), "s_ve")
        bld.wait("gpsimd", "s_ve")
        bld.dma("gpsimd", lat_in.ap()[QLAT + KVR:LATR, so:so + 512], exch[0:64, 0:512], "s_g0")

    bld.wait("gpsimd", "s_g0")
    bld.wait("gpsimd", "s_g1")
    bld.inc("gpsimd", lambda e: e.collective_compute(
        "AllGather", ALU.bypass, replica_groups=rg,
        ins=[lat_in.ap()], outs=[lat_g.ap()]), "s_cc", 1)
    CC_LAT = bld.mark("s_cc")

    bld.wait("sync", "s_cc", CC_LAT)
    exs = [0, 0]
    for b in range(B):
        bld.wait("sync", "s_mm")
        for k in range(2):
            bld.dma("sync", big[:, CKV + k * 2048: CKV + (k + 1) * 2048]
                    .rearrange("p (c d) -> p c d", c=8),
                    lat_v[0:8, QLAT + k * 128:QLAT + (k + 1) * 128,
                          b * 256:(b + 1) * 256].transpose([1, 0, 2]), "s_wS")
        bld.dma("sync", s2[0:64, 0:2048].rearrange("p (c d) -> p c d", c=8),
                lat_v[0:8, QLAT + KVR:LATR, b * 256:(b + 1) * 256].transpose([1, 0, 2]), "s_wS")
        CKV_LD = bld.mark("s_wS")
        for pair in range(2):
            for slab in range(4):
                bld.wait("sync", "s_mm")
                qa_ld = 0
                for k in range(8):
                    qa_ld = bld.dma("sync", big[:, QA + k * 512: QA + (k + 1) * 512]
                            .rearrange("p (c d) -> p c d", c=2),
                            lat_v[2 * slab:2 * slab + 2, k * 128:(k + 1) * 128,
                                  b * 256:(b + 1) * 256].transpose([1, 0, 2]), "s_wS")
                for hh in range(2):
                    hl = pair * 2 + hh
                    pq_ = pp.acq(5 + hh)
                    bld.wait("tensor", "s_wS", qa_ld)
                    for k in range(8):
                        bld.inc("tensor", lambda e, k_=k, hl_=hl, pq=pq_: e.matmul(
                            pq[:, :], wqb[:, k_ * 512 + hl_ * 128: k_ * 512 + (hl_ + 1) * 128],
                            big[:, QA + k_ * 512: QA + (k_ + 1) * 512],
                            start=(k_ == 0), stop=(k_ == 7)), "s_mm")
                    bld.wait("scalar", "s_mm")
                    qfc = QF + hh * 2048 + slab * 512
                    bld.inc("scalar", lambda e, pq=pq_, c=qfc: e.copy(big[0:64, c:c + 512], pq[0:64, :]), "s_sc")
                    bld.inc("scalar", lambda e, pq=pq_: e.copy(pe_scr[0:64, :], pq[64:128, :]), "s_sc")
                    bld.inc("scalar", lambda e, pq=pq_: e.copy(pe_scr[64:96, :], pq[96:128, :]), "s_sc")
                    bld.inc("scalar", lambda e, pq=pq_: e.copy(pe_scr[96:128, :], pq[64:96, :]), "s_sc")
                    pp.rel(5 + hh, "s_sc")
                    bld.wait("vector", "s_sc")
                    sc0 = slab * 512
                    bld.inc("vector", lambda e, s_=sc0: e.tensor_mul(
                        scrA[0:64, 0:512], pe_scr[0:64, :], CS2[0:64, s_:s_ + 512]), "s_ve")
                    bld.inc("vector", lambda e, s_=sc0: e.tensor_mul(
                        scrA[0:64, 512:1024], pe_scr[64:128, :], CS2[64:128, s_:s_ + 512]), "s_ve")
                    bld.inc("vector", lambda e, c=qfc: e.tensor_sub(
                        big[64:96, c:c + 512], scrA[0:32, 0:512], scrA[0:32, 512:1024]), "s_ve")
                    bld.inc("vector", lambda e, c=qfc: e.tensor_add(
                        big[96:128, c:c + 512], scrA[32:64, 0:512], scrA[32:64, 512:1024]), "s_ve")
                    bld.wait("scalar", "s_ve")
            bld.wait("tensor", "s_wS", CKV_LD)
            for slab in range(4):
                pk = pp.acq(5)
                for k in range(2):
                    bld.inc("tensor", lambda e, k_=k, s_=slab, pk=pk, pr=pair: e.matmul(
                        pk[:, :], wkvn[:, k_ * 256 + pr * 128: k_ * 256 + (pr + 1) * 128],
                        big[:, CKV + k_ * 2048 + s_ * 512: CKV + k_ * 2048 + (s_ + 1) * 512],
                        start=(k_ == 0), stop=(k_ == 1)), "s_mm")
                bld.wait("scalar", "s_mm")
                bld.inc("scalar", lambda e, s_=slab, pk=pk: e.copy(
                    big[0:64, KF + s_ * 512: KF + (s_ + 1) * 512], pk[0:64, :]), "s_sc")
                bld.inc("scalar", lambda e, s_=slab, pk=pk: e.copy(
                    big[0:64, KF + 2048 + s_ * 512: KF + 2048 + (s_ + 1) * 512], pk[64:128, :]), "s_sc")
                pp.rel(5, "s_sc")
            for hh in range(2):
                bld.inc("scalar", lambda e, hh_=hh: e.copy(
                    big[64:128, KF + hh_ * 2048: KF + (hh_ + 1) * 2048], s2[0:64, 0:2048]), "s_sc")
            for hh in range(2):
                hl = pair * 2 + hh
                for g4 in range(4):
                    pv = pp.acq(6)
                    for tc4 in range(4):
                        tc = g4 * 4 + tc4
                        for k in range(2):
                            bld.inc("tensor", lambda e, k_=k, t_=tc, j_=tc4, hl_=hl, pv=pv: e.matmul(
                                pv[:, j_ * 128:(j_ + 1) * 128],
                                big[:, CKV + k_ * 2048 + t_ * 128: CKV + k_ * 2048 + (t_ + 1) * 128],
                                wkvv[:, k_ * 512 + hl_ * 128: k_ * 512 + (hl_ + 1) * 128],
                                start=(k_ == 0), stop=(k_ == 1)), "s_mm")
                    bld.wait("scalar", "s_mm")
                    bld.inc("scalar", lambda e, hh_=hh, g_=g4, pv=pv: e.copy(
                        big[:, VT + hh_ * 2048 + g_ * 512: VT + hh_ * 2048 + (g_ + 1) * 512],
                        pv[:, :]), "s_sc")
                    pp.rel(6, "s_sc")
            if DBG == "qkv" and b == 0 and pair == 0:
                return nc, finish()
            bld.wait("tensor", "s_sc")
            for hh in range(2):
                hl = pair * 2 + hh
                pt_ev = [0, 0]
                ptv_ev = [0, 0]
                for sq in range(4):
                    pav = pp.acq(2)
                    psm = pp.acq(3)
                    nt_ = 4 * sq + 4
                    pscs, scm = {}, {}
                    SCB = (0, 1, 4, 5)

                    def em_score(t_, sq_=sq, hh_=hh, pscs=pscs, scm=scm):
                        pscs[t_] = pp.acq(SCB[t_ % 4])
                        scm[t_] = bld.inc("tensor", lambda e, t_=t_, s_=sq_, hh_=hh_,
                                          psc=pscs[t_]: e.matmul(
                            psc[:, :], big[:, KF + hh_ * 2048 + t_ * 128: KF + hh_ * 2048 + (t_ + 1) * 128],
                            big[:, QF + hh_ * 2048 + s_ * 512: QF + hh_ * 2048 + (s_ + 1) * 512],
                            start=True, stop=True), "s_mm")
                    em_score(0)
                    if nt_ > 1:
                        em_score(1)
                    for t in range(nt_):
                        if t + 2 < nt_:
                            em_score(t + 2)
                        psc = pscs.pop(t)
                        bld.wait("scalar", "s_mm", scm[t])
                        bld.wait("scalar", "s_mm", pt_ev[t % 2])
                        bld.wait("scalar", "s_ve", ptv_ev[t % 2])
                        bld.inc("scalar", lambda e, t_=t, psc=psc: e.activation(
                            Pt[:, (t_ % 2) * 512:(t_ % 2) * 512 + 512], psc[:, :], AF.Exp,
                            bias=cst[:, C_NEG30:C_NEG30 + 1], scale=cst[:, C_SMS:C_SMS + 1]), "s_sc")
                        pp.rel(SCB[t % 4], "s_sc")
                        exp_mark = bld.mark("s_sc")
                        d = t - 4 * sq
                        bld.wait("vector", "s_sc", exp_mark)
                        if d >= 0:
                            if d > 0:
                                bld.inc("vector", lambda e, t_=t, d_=d: e.memset(
                                    Pt[:, (t_ % 2) * 512:(t_ % 2) * 512 + 128 * d_], 0.0), "s_ve")
                            bld.inc("vector", lambda e, t_=t, d_=d: e.tensor_mul(
                                Pt[:, (t_ % 2) * 512 + 128 * d_:(t_ % 2) * 512 + 128 * (d_ + 1)],
                                Pt[:, (t_ % 2) * 512 + 128 * d_:(t_ % 2) * 512 + 128 * (d_ + 1)],
                                trib[:, :]), "s_ve")
                        trib_mark = bld.mark("s_ve")
                        if t == 0:
                            bld.inc("vector", lambda e: e.tensor_copy(
                                stats[:, 0:512], Pt[:, 0:512]), "s_ve")
                        else:
                            bld.inc("vector", lambda e, t_=t: e.tensor_add(
                                stats[:, 0:512], stats[:, 0:512],
                                Pt[:, (t_ % 2) * 512:(t_ % 2) * 512 + 512]), "s_ve")
                        ptv_ev[t % 2] = bld.mark("s_ve")
                        if d >= 0:
                            bld.wait("tensor", "s_ve", trib_mark)
                        else:
                            bld.wait("tensor", "s_sc", exp_mark)
                        pt_ev[t % 2] = bld.inc("tensor", lambda e, t_=t, hh_=hh, pav=pav, n_=nt_: e.matmul(
                            pav[:, :], big[:, VT + hh_ * 2048 + t_ * 128: VT + hh_ * 2048 + (t_ + 1) * 128],
                            Pt[:, (t_ % 2) * 512:(t_ % 2) * 512 + 512],
                            start=(t_ == 0), stop=(t_ == n_ - 1)), "s_mm")
                    bld.wait("tensor", "s_ve")
                    bld.inc("tensor", lambda e, psm=psm: e.matmul(
                        psm[0:1, :], onesf[:, 0:1], stats[:, 0:512],
                        start=True, stop=True), "s_mm")
                    bld.wait("scalar", "s_mm")
                    bld.inc("scalar", lambda e, psm=psm: e.copy(MRS[0:1, 0:512], psm[0:1, :]), "s_sc")
                    pp.rel(3, "s_sc")
                    bld.wait("tensor", "s_sc")
                    bcast(MRS[0:1, 0:512], MRS[:, 0:512], 4)
                    bld.inc("vector", lambda e: e.reciprocal(MRS[:, 0:512], MRS[:, 0:512]), "s_ve")
                    bld.wait("vector", "s_sc")
                    es = sq % 2
                    bld.wait("vector", "s_g0" if es == 0 else "s_g1", exs[es])
                    bld.inc("vector", lambda e, pav=pav, es=es: e.tensor_mul(
                        exch[:, es * 512:es * 512 + 512], pav[:, :], MRS[:, 0:512]), "s_ve")
                    pp.rel(2, "s_ve")
                    bld.wait("gpsimd", "s_ve")
                    exs[es] = bld.dma("gpsimd",
                            atin_v[2 * sq:2 * sq + 2, hl * 128:(hl + 1) * 128,
                                   b * 256:(b + 1) * 256].transpose([1, 0, 2]),
                            exch[:, es * 512:es * 512 + 512].rearrange("p (c d) -> p c d", c=2),
                            "s_g0" if es == 0 else "s_g1")

    bld.wait("gpsimd", "s_g0")
    bld.wait("gpsimd", "s_g1")

    def _a2a(e):
        with allow_ici_all_to_all():
            return e.collective_compute(
                "AllToAll", ALU.bypass, replica_groups=rg,
                ins=[at_in.ap()], outs=[at_a2a.ap()])
    bld.inc("gpsimd", _a2a, "s_cc", 1)
    CC_A2A = bld.mark("s_cc")

    bld.wait("sync", "s_cc", CC_A2A)
    for sl in range(2):
        so = sl * 512
        bld.wait("sync", "s_mm")
        ao_ld = bld.dma("sync", big[:, 0:16384].rearrange("p (g w) -> p g w", g=NT),
                at_a2a.ap()[:, so:so + 512].rearrange("(g p) w -> p g w", p=128), "s_wS")
        bld.wait("tensor", "s_wS", ao_ld)

        def epi_wo(mg, pb_, so_=so):
            for j in range(4):
                mt = mg * 4 + j
                bld.inc("vector", lambda e, j_=j, mt_=mt, pb=pb_: e.tensor_add(
                    x_t[:, mt_ * TLOC + so_: mt_ * TLOC + so_ + 512], pb[j_][:, :],
                    x_t[:, mt_ * TLOC + so_: mt_ * TLOC + so_ + 512]), "s_ve")
        wchain_full("WOT", None, None, None, epi_wo)

    if stage == 3:
        return nc, finish()

    for sl in range(2):
        so = sl * 512
        ln_stats(so, RS[:, 0:512], RS[0:1, 0:512], MRS[:, 0:512], MRS[0:1, 0:512])
        if sl == 0:
            bld.inc("vector", lambda e: e.tensor_copy(stats[32:33, 0:512], RS[0:1, 0:512]), "s_ve")
            bld.inc("vector", lambda e: e.tensor_copy(stats[64:65, 0:512], MRS[0:1, 0:512]), "s_ve")
        for k in range(NT):
            xn_gen(k, so, k * 512, RS[:, 0:512], MRS[:, 0:512])
        for k in range(NT):
            for bb in range(2):
                gb = sl * 2 + bb
                bld.inc("vector", lambda e, k_=k, bb_=bb, g_=gb: e.scalar_tensor_tensor(
                    out=sqs[:, 0:256], in0=big[:, k_ * 512 + bb_ * 256: k_ * 512 + bb_ * 256 + 256],
                    scalar=1.0, in1=maskb[:, 0:256], op0=ALU.mult, op1=ALU.mult,
                    accum_out=rpart[:, k_ * 4 + g_: k_ * 4 + g_ + 1]), "s_ve")
    bld.wait("gpsimd", "s_ve")
    bld.dma("gpsimd", rt_in.ap(), rpart[:, :], "s_gc")
    bld.wait("gpsimd", "s_gc")
    bld.inc("gpsimd", lambda e: e.collective_compute(
        "AllReduce", ALU.add, replica_groups=rg,
        ins=[rt_in.ap()], outs=[rt_out.ap()]), "s_cc", 1)
    bld.wait("gpsimd", "s_cc")
    bld.dma("gpsimd", rpart[:, :], rt_out.ap(), "s_gc")
    bld.wait("vector", "s_gc")
    bld.inc("vector", lambda e: e.tensor_scalar(Pt[:, 0:128], rpart[:, :],
            cst[:, C_IPOOL:C_IPOOL + 1], None, ALU.mult), "s_ve")
    bld.dma("sync", dba[:, :].rearrange("p (k e) -> p k e", k=32),
            wv("RWT", 0, D, 0, 8).rearrange("(k p) e -> p k e", p=128), "s_wS")
    pl = pp.acq(1)
    bld.wait("tensor", "s_wS")
    bld.wait("tensor", "s_ve")
    for k in range(NT):
        bld.inc("tensor", lambda e, k_=k: e.matmul(
            pl[0:8, 0:4], dba[:, k_ * 8:(k_ + 1) * 8], Pt[:, k_ * 4:(k_ + 1) * 4],
            start=(k_ == 0), stop=(k_ == NT - 1)), "s_mm")
    DBA_FREE2 = bld.mark("s_mm")
    bld.wait("scalar", "s_mm")
    bld.inc("scalar", lambda e: e.activation(scrB[0:8, 0:4], pl[0:8, 0:4], AF.Identity,
            bias=rbt[:, 0:1]), "s_sc")
    pp.rel(1, "s_sc")
    pt_ = pp.acq(2)
    bld.wait("tensor", "s_sc")
    bld.inc("tensor", lambda e: e.transpose(pt_[0:4, 0:8], scrB[0:8, 0:4], identf[0:8, 0:8]), "s_mm")
    bld.wait("scalar", "s_mm")
    bld.inc("scalar", lambda e: e.copy(scrB[0:4, 8:16], pt_[0:4, 0:8]), "s_sc")
    pp.rel(2, "s_sc")
    bld.wait("vector", "s_sc")
    vops(lambda e: e.reduce_max(scrB[0:4, 16:17], scrB[0:4, 8:16], AX),
         lambda e: e.tensor_scalar(scrB[0:4, 17:18], scrB[0:4, 16:17], -1.0, None, ALU.mult))
    bld.wait("scalar", "s_ve")
    bld.inc("scalar", lambda e: e.activation(scrB[0:4, 8:16], scrB[0:4, 8:16], AF.Exp,
            bias=scrB[0:4, 17:18]), "s_sc")
    bld.wait("scalar", "s_sc")
    bld.wait("vector", "s_sc")
    vops(lambda e: e.reduce_sum(scrB[0:4, 16:17], scrB[0:4, 8:16], AX),
         lambda e: e.reciprocal(scrB[0:4, 16:17], scrB[0:4, 16:17]),
         lambda e: e.tensor_scalar(scrB[0:4, 8:16], scrB[0:4, 8:16],
            scrB[0:4, 16:17], None, ALU.mult))
    pt_ = pp.acq(2)
    bld.wait("tensor", "s_ve")
    bld.inc("tensor", lambda e: e.transpose(pt_[0:8, 0:4], scrB[0:4, 8:16], identf[0:4, 0:4]), "s_mm")
    bld.wait("scalar", "s_mm")
    bld.inc("scalar", lambda e: e.copy(scrB[0:8, 20:24], pt_[0:8, 0:4]), "s_sc")
    pp.rel(2, "s_sc")
    pe_ = pp.acq(3)
    bld.wait("tensor", "s_sc")
    bld.inc("tensor", lambda e: e.matmul(pe_[:, 0:4], expf[:, :], scrB[0:8, 20:24],
            start=True, stop=True), "s_mm")
    bld.wait("scalar", "s_mm")
    bld.inc("scalar", lambda e: e.activation(scrB[:, 24:28], pe_[:, 0:4], AF.Identity,
            scale=cst[:, C_TWO:C_TWO + 1]), "s_sc")
    pp.rel(3, "s_sc")
    bld.wait("vector", "s_sc")

    for sl in (1, 0):
        so = sl * 512
        if sl == 0:
            bcast(stats[32:33, 0:512], RS[:, 0:512], 2, lhs=ones3f[32:33, :])
            bcast(stats[64:65, 0:512], MRS[:, 0:512], 3, lhs=ones3f[64:65, :])
            for k in range(NT):
                xn_gen(k, so, k * 512, RS[:, 0:512], MRS[:, 0:512])
        V_XN = bld.mark("s_ve")
        bld.wait("tensor", "s_ve", V_XN)
        ph = pp.acq(0)
        groups = [(gview("A1T", g4 * 4, 4, 0, 128), 512) for g4 in range(8)]

        def use_a1(g, col, ph=ph):
            last = 0
            for t in range(4):
                k = g * 4 + t
                last = bld.inc("tensor", lambda e, k_=k, c=col + t * 128: e.matmul(
                    ph[:, :], dbw[:, c:c + 128], big[:, k_ * 512:(k_ + 1) * 512],
                    start=(k_ == 0), stop=(k_ == NT - 1)), "s_mm")
            return last
        stream2(groups, use_a1, dbw, 512)
        bld.wait("vector", "s_mm")
        for bb in range(2):
            gb = sl * 2 + bb
            bld.inc("vector", lambda e, bb_=bb, g_=gb, ph=ph: e.tensor_scalar(
                scrA[:, bb_ * 256:(bb_ + 1) * 256], ph[:, bb_ * 256:(bb_ + 1) * 256],
                scrB[:, 24 + g_:25 + g_], None, ALU.mult), "s_ve")
        pp.rel(0, "s_ve")
        V_HW = bld.mark("s_ve")

        def epi_w1(mg, pb_, so_=so):
            for j in range(4):
                mt = mg * 4 + j
                bld.inc("vector", lambda e, j_=j, mt_=mt, pb=pb_: e.scalar_tensor_tensor(
                    out=x_t[:, mt_ * TLOC + so_: mt_ * TLOC + so_ + 512],
                    in0=pb[j_][:, :], scalar=bvec[:, 32 + mt_:33 + mt_],
                    in1=x_t[:, mt_ * TLOC + so_: mt_ * TLOC + so_ + 512],
                    op0=ALU.add, op1=ALU.add), "s_ve")
        wchain_full("W1T", "B1S", scrA[:, 0:512], V_HW, epi_w1)

    return nc, finish()


def _finalize(nc, bld):
    ctx = ExitStack()
    for name in sorted(bld.plan):
        bld.sems[name] = ctx.enter_context(nc.semaphore(name))
    with nc.Block() as block:
        for eng in Bld.ENGS:
            def section(e, _ops=tuple(bld.ops[eng])):
                for fn in _ops:
                    fn(e)
            getattr(block, eng)(section)
    ctx.close()
    return True


_CACHE = {}


def kernel(**inputs):
    in_maps, offs, wps = _prep_host(inputs)
    key = (wps, STAGE)
    if key not in _CACHE:
        nc, _ = build_program(offs, wps, STAGE)
        _CACHE[key] = nc
    nc = _CACHE[key]
    res = run_bass_kernel_spmd(nc, in_maps, core_ids=list(range(NCORES)))
    out = np.empty((B, S, D), np.float32)
    for c in range(NCORES):
        xo = res.results[c]["xout"]
        out[:, SCH * c:SCH * (c + 1), :] = xo.T.reshape(B, SCH, D)
    return out


# version 12 (final)
# speedup vs baseline: 1.0203x; 1.0203x over previous
import math
import numpy as np
import ml_dtypes
import concourse.bass as bass
import concourse.mybir as mybir
from concourse.bass_utils import run_bass_kernel_spmd
from concourse.aot_env import allow_ici_all_to_all
from contextlib import ExitStack

F32 = mybir.dt.float32
BF16 = mybir.dt.bfloat16
AF = mybir.ActivationFunctionType
ALU = mybir.AluOpType
AX = mybir.AxisListType.X

NCORES = 8
B, S, D = 4, 2048, 4096
SCH = S // NCORES
TLOC = B * SCH
NT = D // 128
HLOC = 4
NOPE, ROPE, VD, QHD, KVR, QLAT = 64, 64, 128, 128, 256, 1024
Q_START, Q_END = 611, 2047
LATR = QLAT + KVR + ROPE

C_NEG30, C_SMS, C_L0SC, C_TWO, C_EPS5, C_EPS6, C_I4096, C_IPOOL, C_I1024, C_I256, C_ZERO = range(11)
NCONST = 16

STAGE = 4
DBG = None



def _yarn_cos_sin(seq_len, dim=64, base=10000.0, factor=128.0, beta_fast=32.0,
                  beta_slow=1.0, orig_max=8192, mscale=1.0, mscale_all_dim=1.0):
    def corr_dim(r):
        return dim * math.log(orig_max / (r * 2 * math.pi)) / (2 * math.log(base))
    low = max(math.floor(corr_dim(beta_fast)), 0)
    high = min(math.ceil(corr_dim(beta_slow)), dim - 1)
    hi = high + 0.001 if low == high else float(high)
    ramp = np.clip((np.arange(dim // 2, dtype=np.float32) - low) / (hi - low), 0.0, 1.0)
    inv_freq_mask = 1.0 - ramp
    freq_extra = 1.0 / base ** (np.arange(0, dim, 2, dtype=np.float32) / dim)
    freq_inter = freq_extra / factor
    inv_freq = freq_inter * (1.0 - inv_freq_mask) + freq_extra * inv_freq_mask
    t = np.arange(seq_len, dtype=np.float32)
    freqs = np.outer(t, inv_freq)
    emb = np.concatenate([freqs, freqs], axis=-1)

    def gm(s, m):
        return 1.0 if s <= 1 else 0.1 * m * math.log(s) + 1.0
    ms = gm(factor, mscale) / gm(factor, mscale_all_dim)
    return np.cos(emb) * ms, np.sin(emb) * ms


def _deint(lo, n=64):
    return [lo + 2 * i for i in range(n // 2)] + [lo + 1 + 2 * i for i in range(n // 2)]


def _prep_host(inputs):
    f = lambda k: np.asarray(inputs[k], dtype=np.float32)
    x_full = f("L0_attn_x")

    pieces = [[], []]
    offs = {}

    def add(wid, name, arr):
        arr = np.ascontiguousarray(arr, dtype=np.float32)
        offs[name] = (wid, sum(p.size for p in pieces[wid]), arr.shape)
        pieces[wid].append(arr.reshape(-1))

    add(0, "W0T", f("L0_ffn_W_base").T)
    add(0, "G0T", f("L0_ffn_W_gate").T)
    add(0, "A0T", f("L0_ffn_A").transpose(2, 0, 1).reshape(D, 128))
    add(0, "B0S", f("L0_ffn_B").transpose(0, 2, 1).reshape(128, D))
    add(1, "QAT", f("L1_attn_wq_a").T)
    wkv = f("L1_attn_wkv_a")
    add(1, "KAT", np.concatenate([wkv[0:KVR], wkv[_deint(KVR)]], axis=0).T)
    add(1, "WOT", f("L1_attn_wo").T)
    add(1, "W1T", f("L1_ffn_W").T)
    add(1, "A1T", f("L1_ffn_A").transpose(2, 0, 1).reshape(D, 128))
    add(1, "B1S", f("L1_ffn_B").transpose(0, 2, 1).reshape(128, D))
    add(1, "RWT", f("L1_ffn_router_W").T)
    walls, wps = [], []
    for wid in range(2):
        wall = np.concatenate(pieces[wid])
        lp = ((wall.size + 1024 * NCORES - 1) // (1024 * NCORES)) * (1024 * NCORES)
        walls.append(np.pad(wall, (0, lp - wall.size)))
        wps.append(lp // NCORES // 128)

    wqbm = f("L1_attn_wq_b")
    wkvb = f("L1_attn_wkv_b")
    per_core_w = []
    for c in range(NCORES):
        qrows, nrows, vrows = [], [], []
        for hl in range(HLOC):
            h = HLOC * c + hl
            qrows += list(range(h * QHD, h * QHD + NOPE)) + _deint(h * QHD + NOPE)
            nrows += list(range(h * 192, h * 192 + NOPE))
            vrows += list(range(h * 192 + NOPE, h * 192 + 192))
        per_core_w.append((np.ascontiguousarray(wqbm[qrows].T),
                           np.ascontiguousarray(wkvb[nrows].T),
                           np.ascontiguousarray(wkvb[vrows].T)))

    bvec = np.zeros((128, 64), np.float32)
    bvec[:, 0:32] = f("L0_ffn_b_base").reshape(32, 128).T
    bvec[:, 32:64] = f("L1_ffn_b").reshape(32, 128).T
    lnw = np.zeros((128, 16), np.float32)
    lnw[:, 0:8] = f("L1_attn_q_a_ln_w").reshape(8, 128).T
    lnw[:, 8:10] = f("L1_attn_kv_a_ln_w").reshape(2, 128).T
    rb = f("L1_ffn_router_b").reshape(8, 1)

    m_ = 0.1 * 1.0 * math.log(128.0) + 1.0
    consts = np.zeros((128, NCONST), np.float32)
    for i, v in [(C_NEG30, -30.0), (C_SMS, (QHD ** -0.5) * m_ * m_),
                 (C_L0SC, 1.0 / math.sqrt(2048.0)), (C_TWO, 2.0),
                 (C_EPS5, 1e-5), (C_EPS6, 1e-6), (C_I4096, 1.0 / 4096.0),
                 (C_IPOOL, 1.0 / (Q_END - Q_START)), (C_I1024, 1.0 / 1024.0),
                 (C_I256, 1.0 / 256.0), (C_ZERO, 0.0)]:
        consts[:, i] = v

    cosf, sinf = _yarn_cos_sin(S)
    CS2 = np.zeros((128, S), np.float32)
    CS2[0:32] = cosf.T[0:32]
    CS2[32:64] = cosf.T[32:64]
    CS2[64:96] = sinf.T[0:32]
    CS2[96:128] = sinf.T[32:64]

    identf = np.eye(128, dtype=np.float32)
    trib = np.triu(np.ones((128, 128), np.float32)).astype(ml_dtypes.bfloat16)
    expf = np.zeros((8, 128), np.float32)
    for e in range(8):
        expf[e, e * 16:(e + 1) * 16] = 1.0
    ones1f = np.ones((1, 128), np.float32)
    ones3f = np.ones((96, 128), np.float32)
    onesb = np.ones((128, 1), ml_dtypes.bfloat16)
    onesf = np.ones((128, 1), np.float32)

    in_maps = []
    for c in range(NCORES):
        s0 = SCH * c
        x_c = np.ascontiguousarray(x_full[:, s0:s0 + SCH, :].reshape(TLOC, D).T)
        wallpA = np.ascontiguousarray(
            walls[0][c * 128 * wps[0]:(c + 1) * 128 * wps[0]].reshape(128, wps[0]))
        wallpB = np.ascontiguousarray(
            walls[1][c * 128 * wps[1]:(c + 1) * 128 * wps[1]].reshape(128, wps[1]))
        mask = np.zeros((128, SCH), np.float32)
        for ds in range(SCH):
            if Q_START <= s0 + ds < Q_END:
                mask[:, ds] = 1.0
        in_maps.append(dict(
            xin=x_c, wallpA=wallpA, wallpB=wallpB,
            wqbT=per_core_w[c][0], wkvnT=per_core_w[c][1], wkvvT=per_core_w[c][2],
            bvec=bvec, lnw=lnw, rb=rb, consts=consts,
            CS=CS2.astype(ml_dtypes.bfloat16),
            CSL=np.ascontiguousarray(CS2[:, s0:s0 + SCH]).astype(ml_dtypes.bfloat16),
            identf=identf, trib=trib, expf=expf, ones1f=ones1f, ones3f=ones3f,
            onesb=onesb, onesf=onesf, maskb=mask.astype(ml_dtypes.bfloat16)))
    return in_maps, offs, tuple(wps)



class Bld:
    ENGS = ("sync", "gpsimd", "tensor", "scalar", "vector")

    def __init__(self, nc):
        self.nc = nc
        self.ops = {e: [] for e in self.ENGS}
        self.plan = {}
        self.sems = {}

    def emit(self, eng, fn):
        self.ops[eng].append(fn)

    def inc(self, eng, build_fn, sname, k=1):
        self.plan[sname] = self.plan.get(sname, 0) + k
        self.emit(eng, lambda e, _b=build_fn, _s=sname, _k=k: _b(e).then_inc(self.sems[_s], _k))
        return self.plan[sname]

    def dma(self, eng, out, in_, sname):
        return self.inc(eng, lambda e, o=out, i=in_: e.dma_start(out=o, in_=i), sname, 16)

    def wait(self, eng, sname, val=None):
        v = self.plan.get(sname, 0) if val is None else val
        if v > 0:
            self.emit(eng, lambda e, _s=sname, _v=v: e.wait_ge(self.sems[_s], _v))

    def mark(self, sname):
        return self.plan.get(sname, 0)


class Pp:
    def __init__(self, bld, banks):
        self.bld, self.banks = bld, banks
        self.ev = [None] * len(banks)

    def acq(self, i):
        if self.ev[i] is not None:
            self.bld.wait("tensor", self.ev[i][0], self.ev[i][1])
            self.ev[i] = None
        return self.banks[i]

    def rel(self, i, sem):
        self.ev[i] = (sem, self.bld.mark(sem))



def build_program(offs, wps, stage):
    nc = bass.Bass("TRN2", target_bir_lowering=False, debug=False, detect_race_conditions=False)

    def din(name, shape, dt):
        return nc.dram_tensor(name, shape, dt, kind="ExternalInput").ap()

    xin = din("xin", [D, TLOC], F32)
    wallpA = din("wallpA", [128, wps[0]], F32)
    wallpB = din("wallpB", [128, wps[1]], F32)
    wqbT_h = din("wqbT", [QLAT, 512], F32)
    wkvnT_h = din("wkvnT", [KVR, 256], F32)
    wkvvT_h = din("wkvvT", [KVR, 512], F32)
    bvec_h = din("bvec", [128, 64], F32)
    lnw_h = din("lnw", [128, 16], F32)
    rb_h = din("rb", [8, 1], F32)
    consts_h = din("consts", [128, NCONST], F32)
    CS_h = din("CS", [128, S], BF16)
    CSL_h = din("CSL", [128, SCH], BF16)
    identf_h = din("identf", [128, 128], F32)
    trib_h = din("trib", [128, 128], BF16)
    expf_h = din("expf", [8, 128], F32)
    ones1f_h = din("ones1f", [1, 128], F32)
    ones3f_h = din("ones3f", [96, 128], F32)
    onesb_h = din("onesb", [128, 1], BF16)
    onesf_h = din("onesf", [128, 1], F32)
    maskb_h = din("maskb", [128, SCH], BF16)

    xout = nc.dram_tensor("xout", [D, TLOC], F32, kind="ExternalOutput").ap()

    wall_inA = nc.dram_tensor("wall_inA", [128, wps[0]], BF16)
    wall_gA = nc.dram_tensor("wall_gA", [NCORES * 128, wps[0]], BF16, addr_space="Shared")
    wall_inB = nc.dram_tensor("wall_inB", [128, wps[1]], BF16)
    wall_gB = nc.dram_tensor("wall_gB", [NCORES * 128, wps[1]], BF16, addr_space="Shared")
    sc_in = nc.dram_tensor("sc_in", [128, 320], F32)
    sc_out = nc.dram_tensor("sc_out", [128, 320], F32)
    lat_in = nc.dram_tensor("lat_in", [LATR, TLOC], BF16)
    lat_g = nc.dram_tensor("lat_g", [NCORES * LATR, TLOC], BF16, addr_space="Shared")
    at_in = nc.dram_tensor("at_in", [NCORES * HLOC * VD, TLOC], BF16)
    at_a2a = nc.dram_tensor("at_a2a", [NCORES * HLOC * VD, TLOC], BF16)
    rt_in = nc.dram_tensor("rt_in", [128, 128], F32)
    rt_out = nc.dram_tensor("rt_out", [128, 128], F32)

    wall_flats = [wall_gA.ap().rearrange("a b -> (a b)"),
                  wall_gB.ap().rearrange("a b -> (a b)")]

    def wv(name, r0, r1, c0, c1):
        wid, off, shp = offs[name]
        return (wall_flats[wid][off:off + shp[0] * shp[1]]
                .rearrange("(a b) -> a b", b=shp[1])[r0:r1, c0:c1])

    lat_v = lat_g.ap().rearrange("(r l) t -> r l t", l=LATR)
    atin_v = at_in.ap().rearrange("(d r) t -> d r t", r=HLOC * VD)

    a = nc.alloc_sbuf_tensor
    x_t = a("sb_x", [128, NT * TLOC], F32)
    big = a("sb_big", [128, 16384], BF16)
    bigf = big.bitcast(F32)
    s2 = a("sb_s2", [128, 5120], BF16)
    RS = a("sb_RS", [128, 512], F32)
    MRS = a("sb_MRS", [128, 512], F32)
    stats = a("sb_stats", [128, 512], F32)
    wqb = a("sb_wqb", [128, 4096], BF16)
    wkvn = a("sb_wkvn", [128, 512], BF16)
    wkvv = a("sb_wkvv", [128, 1024], BF16)
    dbw = a("sb_dbw", [128, 1024], BF16)
    dba = a("sb_dba", [128, 256], BF16)
    sqs = a("sb_sqs", [128, 1024], BF16)
    Pt = a("sb_P", [128, 1024], BF16)
    scrA = a("sb_scrA", [128, 1024], BF16)
    scrB = a("sb_scrB", [128, 256], F32)
    pe_scr = a("sb_pe", [128, 512], BF16)
    exch = a("sb_exch", [128, 1024], BF16)
    rpart = a("sb_rpart", [128, 128], F32)
    CS2 = a("sb_CS2", [128, S], BF16)
    CSL = a("sb_CSL", [128, SCH], BF16)
    maskb = a("sb_maskb", [128, SCH], BF16)
    identf = a("sb_identf", [128, 128], F32)
    trib = a("sb_trib", [128, 128], BF16)
    expf = a("sb_expf", [8, 128], F32)
    ones1f = a("sb_ones1f", [1, 128], F32)
    ones3f = a("sb_ones3f", [96, 128], F32)
    onesb = a("sb_onesb", [128, 1], BF16)
    onesf = a("sb_onesf", [128, 1], F32)
    cst = a("sb_cst", [128, NCONST], F32)
    bvec = a("sb_bvec", [128, 64], F32)
    lnw = a("sb_lnw", [128, 16], F32)
    rbt = a("sb_rb", [8, 1], F32)
    assert nc.sbuf_bytes_remaining >= 0, nc.sbuf_bytes_remaining

    SFULL = bigf[:, 1024:1536]
    SACC = bigf[:, 1536:1856]
    RS2 = bigf[:, 2048:2560]
    MRS2 = bigf[:, 2560:3072]

    QA = 0
    KF = 0
    QF = 4096
    VT = 8192
    CKV = 12288

    ps = [nc.alloc_psum_tensor(f"ps{i}", [128, 512], F32) for i in range(8)]
    bld = Bld(nc)
    pp = Pp(bld, ps)
    rg = [list(range(NCORES))]

    bld.dma("sync", x_t[:, :].rearrange("p (t c) -> p t c", t=NT),
            xin.rearrange("(t p) c -> p t c", p=128), "s_x")
    for dst, src in [(cst, consts_h), (bvec, bvec_h), (lnw, lnw_h), (rbt, rb_h),
                     (CS2, CS_h), (CSL, CSL_h), (identf, identf_h), (trib, trib_h),
                     (expf, expf_h), (ones1f, ones1f_h), (ones3f, ones3f_h),
                     (onesb, onesb_h), (onesf, onesf_h), (maskb, maskb_h)]:
        bld.dma("sync", dst[:, :], src, "s_x")
    bld.dma("gpsimd", wqb[:, :].rearrange("p (k m) -> p k m", k=8),
            wqbT_h.rearrange("(k p) m -> p k m", p=128), "s_k")
    bld.dma("gpsimd", wkvn[:, :].rearrange("p (k m) -> p k m", k=2),
            wkvnT_h.rearrange("(k p) m -> p k m", p=128), "s_k")
    bld.dma("gpsimd", wkvv[:, :].rearrange("p (k m) -> p k m", k=2),
            wkvvT_h.rearrange("(k p) m -> p k m", p=128), "s_k")

    slot_ev = [0, 0]
    i = 0
    wall_marks = []
    for wall_src, wall_dst, wp in ((wallpA, wall_inA, wps[0]),
                                   (wallpB, wall_inB, wps[1])):
        for c0 in range(0, wp, 2048):
            c1 = min(c0 + 2048, wp)
            sl_ = (i % 2) * 2048
            gsem = "s_g0" if i % 2 == 0 else "s_g1"
            if i >= 2:
                bld.wait("gpsimd", gsem, slot_ev[i % 2])
            bld.dma("gpsimd", s2[:, sl_:sl_ + (c1 - c0)], wall_src[:, c0:c1], "s_k2")
            bld.wait("gpsimd", "s_k2", bld.mark("s_k2"))
            slot_ev[i % 2] = bld.dma("gpsimd", wall_dst.ap()[:, c0:c1],
                                     s2[:, sl_:sl_ + (c1 - c0)], gsem)
            i += 1
        wall_marks.append((bld.mark("s_g0"), bld.mark("s_g1")))

    for eng in ("tensor", "scalar", "vector"):
        bld.wait(eng, "s_x")
        bld.wait(eng, "s_k", 48)

    def vops(*fns):
        for fn in fns:
            bld.inc("vector", fn, "s_ve")
            bld.wait("vector", "s_ve")
    sq_ev = [0, 0]

    def ln_stats(so, rs_full, rs_row, mrs_full, mrs_row):
        bld.wait("scalar", "s_ve")
        bld.wait("tensor", "s_ve")
        pm, pq = pp.acq(0), pp.acq(1)
        for k in range(NT):
            xs = x_t[:, k * TLOC + so: k * TLOC + so + 512]
            sl_ = (k % 2) * 512
            if k >= 2:
                bld.wait("scalar", "s_mm", sq_ev[k % 2])
            bld.inc("scalar", lambda e, o=sqs[:, sl_:sl_ + 512], i=xs:
                    e.activation(o, i, AF.Square), "s_sc")
            bld.wait("tensor", "s_sc")
            bld.inc("tensor", lambda e, o=pm[0:1, :], i=xs, k_=k:
                    e.matmul(o, onesf[:, 0:1], i, start=(k_ == 0), stop=(k_ == NT - 1)), "s_mm")
            sq_ev[k % 2] = bld.inc("tensor",
                    lambda e, o=pq[0:1, :], i=sqs[:, sl_:sl_ + 512], k_=k:
                    e.matmul(o, onesb[:, 0:1], i, start=(k_ == 0), stop=(k_ == NT - 1)), "s_mm")
        bld.wait("vector", "s_mm")
        A = stats[0:1, 0:512]
        bld.inc("vector", lambda e: e.tensor_scalar(A, pm[0:1, :], cst[0:1, C_I4096:C_I4096 + 1], None, ALU.mult), "s_ve")
        bld.inc("vector", lambda e: e.tensor_scalar(mrs_row, pq[0:1, :], cst[0:1, C_I4096:C_I4096 + 1], None, ALU.mult), "s_ve")
        pp.rel(0, "s_ve"); pp.rel(1, "s_ve")
        bld.inc("vector", lambda e: e.tensor_mul(rs_row, A, A), "s_ve")
        bld.inc("vector", lambda e: e.tensor_sub(mrs_row, mrs_row, rs_row), "s_ve")
        bld.wait("scalar", "s_ve")
        bld.inc("scalar", lambda e: e.activation(rs_row, mrs_row, AF.Sqrt, bias=cst[0:1, C_EPS5:C_EPS5 + 1]), "s_sc")
        bld.wait("vector", "s_sc")
        bld.inc("vector", lambda e: e.reciprocal(rs_row, rs_row), "s_ve")
        bld.inc("vector", lambda e: e.tensor_mul(mrs_row, A, rs_row), "s_ve")
        bcast(rs_row, rs_full, 2)
        bcast(mrs_row, mrs_full, 3)
        bld.wait("vector", "s_sc")

    def bcast(row, full, bank, lhs=None):
        pb = pp.acq(bank)
        bld.wait("tensor", "s_ve")
        L = ones1f[0:1, :] if lhs is None else lhs
        bld.inc("tensor", lambda e, r=row, L=L: e.matmul(pb[:, :], L, r, start=True, stop=True), "s_mm")
        bld.wait("scalar", "s_mm")
        bld.inc("scalar", lambda e, f=full: e.copy(f, pb[:, :]), "s_sc")
        pp.rel(bank, "s_sc")
        bld.wait("vector", "s_sc")

    def xn_gen(k, so, dst, rs_full, mrs_full):
        xs = x_t[:, k * TLOC + so: k * TLOC + so + 512]
        o = big[:, dst:dst + 512]
        bld.inc("vector", lambda e, o=o, xs=xs, r=rs_full: e.tensor_mul(o, xs, r), "s_ve")
        bld.inc("vector", lambda e, o=o, m=mrs_full: e.tensor_sub(o, o, m), "s_ve")

    def stream2(groups, consume, buf, slotw):
        n = len(groups)
        ldv = [0] * n
        ent = bld.mark("s_mm")
        used = [ent, ent]
        for g in range(n):
            sl_ = g % 2
            q = "sync" if sl_ == 0 else "gpsimd"
            bld.wait(q, "s_mm", used[sl_])
            ap_, w_ = groups[g]
            ldv[g] = bld.dma(q, buf[:, sl_ * slotw: sl_ * slotw + w_]
                             .rearrange("p (g w) -> p g w", w=ap_.shape[-1]), ap_,
                             "s_w0" if sl_ == 0 else "s_w1")
            if g >= 1:
                kk = g - 1
                bld.wait("tensor", "s_w0" if kk % 2 == 0 else "s_w1", ldv[kk])
                used[kk % 2] = consume(kk, (kk % 2) * slotw)
        kk = n - 1
        bld.wait("tensor", "s_w0" if kk % 2 == 0 else "s_w1", ldv[kk])
        used[kk % 2] = consume(kk, (kk % 2) * slotw)

    def gview(name, k0, ntile, c0, c1):
        return (wv(name, k0 * 128, (k0 + ntile) * 128, c0, c1)
                .rearrange("(g p) w -> p g w", p=128))

    def wchain_full(wname, bname, rhs_last, v_hw, epi):
        groups, meta = [], []
        for mg in range(8):
            for g8 in range(8):
                groups.append((gview(wname, g8 * 4, 4, mg * 512, (mg + 1) * 512), 2048))
                meta.append((mg, g8 * 4))
            if bname is not None:
                groups.append((gview(bname, 0, 1, mg * 512, (mg + 1) * 512), 512))
                meta.append((mg, -1))
        pbs = {}

        def consume(g, col):
            mg, k0 = meta[g]
            if k0 == 0:
                bs = 4 if mg % 2 == 0 else 0
                pbs[mg] = [pp.acq(bs + j) for j in range(4)]
            pb_ = pbs[mg]
            last = 0
            if k0 >= 0:
                for t in range(4):
                    k = k0 + t
                    stop = (bname is None and k == NT - 1)
                    for j in range(4):
                        last = bld.inc("tensor", lambda e, c=col + t * 512, j_=j,
                                       k_=k, pb=pb_, st=stop: e.matmul(
                            pb[j_][:, :], s2[:, c + j_ * 128:c + (j_ + 1) * 128],
                            big[:, k_ * 512:(k_ + 1) * 512],
                            start=(k_ == 0), stop=st), "s_mm")
            else:
                bld.wait("tensor", "s_ve", v_hw)
                for j in range(4):
                    last = bld.inc("tensor", lambda e, c=col, j_=j, pb=pb_: e.matmul(
                        pb[j_][:, :], s2[:, c + j_ * 128:c + (j_ + 1) * 128],
                        rhs_last, start=False, stop=True), "s_mm")
            if k0 == -1 or (bname is None and k0 == 28):
                bld.wait("vector", "s_mm")
                epi(mg, pb_)
                bs = 4 if mg % 2 == 0 else 0
                for j in range(4):
                    pp.rel(bs + j, "s_ve")
            return last
        stream2(groups, consume, s2, 2048)

    def finish():
        bld.wait("sync", "s_ve")
        bld.wait("sync", "s_sc")
        bld.wait("sync", "s_mm")
        bld.dma("sync", xout.rearrange("(t p) c -> p t c", p=128),
                x_t[:, :].rearrange("p (t c) -> p t c", t=NT), "s_st")
        bld.wait("sync", "s_st")
        return _finalize(nc, bld)

    ln_stats(0, RS[:, 0:512], RS[0:1, 0:512], MRS[:, 0:512], MRS[0:1, 0:512])
    ln_stats(512, RS2, RS2[0:1, :], MRS2, MRS2[0:1, :])

    PAIRS = [(n, m) for n in range(B) for m in range(n, B)]
    pidx = {p: i for i, p in enumerate(PAIRS)}

    xn_ev = [0, 0]
    for k in range(NT):
        dst = (k % 2) * 1024
        if k >= 2:
            bld.wait("vector", "s_ve", xn_ev[k % 2])
        xn_gen(k, 0, dst, RS[:, 0:512], MRS[:, 0:512])
        xn_gen(k, 512, dst + 512, RS2, MRS2)
        for (n, m) in PAIRS:
            col = k * 10 + pidx[(n, m)]
            bld.inc("vector", lambda e, n_=n, m_=m, d=dst, c=col:
                    e.scalar_tensor_tensor(
                        out=sqs[:, 0:256], in0=big[:, d + n_ * 256: d + n_ * 256 + 256],
                        scalar=1.0, in1=big[:, d + m_ * 256: d + m_ * 256 + 256],
                        op0=ALU.mult, op1=ALU.mult, accum_out=SACC[:, c:c + 1]), "s_ve")
        xn_ev[k % 2] = bld.mark("s_ve")

    bld.wait("sync", "s_ve")
    bld.dma("sync", sc_in.ap(), SACC, "s_gc")
    bld.wait("gpsimd", "s_gc")
    bld.inc("gpsimd", lambda e: e.collective_compute(
        "AllReduce", ALU.add, replica_groups=rg,
        ins=[sc_in.ap()], outs=[sc_out.ap()]), "s_cc", 1)
    CC_SCORE = bld.mark("s_cc")
    bld.wait("gpsimd", "s_g0", wall_marks[0][0])
    bld.wait("gpsimd", "s_g1", wall_marks[0][1])
    bld.inc("gpsimd", lambda e: e.collective_compute(
        "AllGather", ALU.bypass, replica_groups=rg,
        ins=[wall_inA.ap()], outs=[wall_gA.ap()]), "s_cc", 1)
    CC_WALL = bld.mark("s_cc")
    bld.wait("gpsimd", "s_g0", wall_marks[1][0])
    bld.wait("gpsimd", "s_g1", wall_marks[1][1])
    bld.inc("gpsimd", lambda e: e.collective_compute(
        "AllGather", ALU.bypass, replica_groups=rg,
        ins=[wall_inB.ap()], outs=[wall_gB.ap()]), "s_cc", 1)
    CC_WALLB = bld.mark("s_cc")
    bld.wait("sync", "s_cc", CC_SCORE)
    bld.dma("sync", SACC, sc_out.ap(), "s_gc")
    bld.wait("vector", "s_gc")

    for n in range(B):
        for m in range(B):
            pi = pidx[(n, m) if n <= m else (m, n)]
            bld.inc("vector", lambda e, n_=n, m_=m, p_=pi: e.tensor_copy(
                SFULL[:, n_ * 4 + m_::16], SACC[:, p_::10]), "s_ve")
    M1 = scrB[:, 0:128]
    S1 = scrB[:, 128:256]
    bld.inc("vector", lambda e: e.tensor_max(M1, SFULL[:, 0::4], SFULL[:, 1::4]), "s_ve")
    bld.inc("vector", lambda e: e.tensor_max(M1, M1, SFULL[:, 2::4]), "s_ve")
    bld.inc("vector", lambda e: e.tensor_max(M1, M1, SFULL[:, 3::4]), "s_ve")
    for m in range(B):
        bld.inc("vector", lambda e, m_=m: e.tensor_sub(SFULL[:, m_::4], SFULL[:, m_::4], M1), "s_ve")
    bld.wait("scalar", "s_ve")
    bld.inc("scalar", lambda e: e.activation(SFULL[:, :], SFULL[:, :], AF.Exp,
            scale=cst[:, C_L0SC:C_L0SC + 1]), "s_sc")
    bld.wait("vector", "s_sc")
    bld.inc("vector", lambda e: e.tensor_add(S1, SFULL[:, 0::4], SFULL[:, 1::4]), "s_ve")
    bld.inc("vector", lambda e: e.tensor_add(S1, S1, SFULL[:, 2::4]), "s_ve")
    bld.inc("vector", lambda e: e.tensor_add(S1, S1, SFULL[:, 3::4]), "s_ve")
    bld.inc("vector", lambda e: e.reciprocal(S1, S1), "s_ve")
    for m in range(B):
        bld.inc("vector", lambda e, m_=m: e.tensor_mul(SFULL[:, m_::4], SFULL[:, m_::4], S1), "s_ve")
    bld.wait("vector", "s_ve")
    for n in range(B):
        bld.inc("vector", lambda e, n_=n: e.tensor_scalar(
            SFULL[:, n_ * 5::16], SFULL[:, n_ * 5::16], 1.0, None, ALU.add), "s_ve")
    bld.wait("vector", "s_ve")

    for k in range(NT):
        dst = (k % 2) * 1024
        bld.wait("vector", "s_ve", xn_ev[k % 2])
        xn_gen(k, 0, dst, RS[:, 0:512], MRS[:, 0:512])
        xn_gen(k, 512, dst + 512, RS2, MRS2)
        for n in range(B):
            for m in range(B):
                col = k * 16 + n * 4 + m
                bld.inc("vector", lambda e, n_=n, m_=m, d=dst, c=col, k_=k:
                        e.scalar_tensor_tensor(
                            out=x_t[:, k_ * TLOC + n_ * 256: k_ * TLOC + n_ * 256 + 256],
                            in0=big[:, d + m_ * 256: d + m_ * 256 + 256],
                            scalar=SFULL[:, c:c + 1],
                            in1=x_t[:, k_ * TLOC + n_ * 256: k_ * TLOC + n_ * 256 + 256],
                            op0=ALU.mult, op1=ALU.add), "s_ve")
        xn_ev[k % 2] = bld.mark("s_ve")

    if stage == 1:
        return nc, finish()

    bld.wait("sync", "s_cc", CC_WALL)
    bld.wait("gpsimd", "s_cc", CC_WALL)
    bld.wait("sync", "s_g0", wall_marks[1][0])
    bld.wait("sync", "s_g1", wall_marks[1][1])

    def moe_ffn(layer):
        pass

    for sl in range(2):
        so = sl * 512
        ln_stats(so, RS[:, 0:512], RS[0:1, 0:512], MRS[:, 0:512], MRS[0:1, 0:512])
        for k in range(NT):
            xn_gen(k, so, k * 512, RS[:, 0:512], MRS[:, 0:512])
        V_XN = bld.mark("s_ve")

        bld.dma("sync", dba[:, :].rearrange("p (k e) -> p k e", k=32),
                wv("G0T", 0, D, 0, 8).rearrange("(k p) e -> p k e", p=128), "s_wS")
        pg = pp.acq(1)
        bld.wait("tensor", "s_wS")
        bld.wait("tensor", "s_ve", V_XN)
        for k in range(NT):
            bld.inc("tensor", lambda e, k_=k: e.matmul(
                pg[0:8, :], dba[:, k_ * 8:(k_ + 1) * 8], big[:, k_ * 512:(k_ + 1) * 512],
                start=(k_ == 0), stop=(k_ == NT - 1)), "s_mm")
        DBA_FREE = bld.mark("s_mm")
        bld.wait("scalar", "s_mm")
        bld.inc("scalar", lambda e: e.copy(stats[0:8, 0:512], pg[0:8, :]), "s_sc")
        pp.rel(1, "s_sc")
        pt = pp.acq(2)
        bld.wait("tensor", "s_sc")
        for q in range(4):
            bld.inc("tensor", lambda e, q_=q: e.transpose(
                pt[:, q_ * 8:(q_ + 1) * 8], stats[0:8, q_ * 128:(q_ + 1) * 128],
                identf[0:8, 0:8]), "s_mm")
        bld.wait("scalar", "s_mm")
        bld.inc("scalar", lambda e: e.copy(scrB[:, 0:32], pt[:, 0:32]), "s_sc")
        pp.rel(2, "s_sc")
        bld.wait("vector", "s_sc")
        for q in range(4):
            lt = scrB[:, q * 8:(q + 1) * 8]
            m1 = scrB[:, 32 + q:33 + q]
            m1n = scrB[:, 36 + q:37 + q]
            m2 = scrB[:, 40 + q:41 + q]
            ms = scrB[:, 44:52]
            ex = scrB[:, 52:60]
            sm = scrB[:, 60:61]
            vops(lambda e, lt=lt, m1=m1: e.reduce_max(m1, lt, AX),
                 lambda e, m1=m1, m1n=m1n: e.tensor_scalar(m1n, m1, -1.0, None, ALU.mult),
                 lambda e, lt=lt, m1=m1, ms=ms: e.tensor_single_scalar(ms, lt, m1, ALU.is_ge),
                 lambda e, lt=lt, ms=ms, ex=ex: e.scalar_tensor_tensor(
                     out=ex, in0=ms, scalar=-1e30, in1=lt, op0=ALU.mult, op1=ALU.add),
                 lambda e, ex=ex, m2=m2: e.reduce_max(m2, ex, AX),
                 lambda e, lt=lt, m2=m2, ms=ms: e.tensor_single_scalar(ms, lt, m2, ALU.is_ge))
            bld.wait("scalar", "s_ve")
            bld.inc("scalar", lambda e, lt=lt, ex=ex, m1n=m1n: e.activation(ex, lt, AF.Exp, bias=m1n), "s_sc")
            bld.wait("scalar", "s_sc")
            bld.wait("vector", "s_sc")
            vops(lambda e, ex=ex, ms=ms: e.tensor_mul(ex, ex, ms),
                 lambda e, ex=ex, sm=sm: e.reduce_sum(sm, ex, AX),
                 lambda e, sm=sm: e.reciprocal(sm, sm),
                 lambda e, ex=ex, sm=sm: e.tensor_scalar(ex, ex, sm, None, ALU.mult))
            pt2 = pp.acq(2)
            bld.wait("tensor", "s_ve")
            bld.inc("tensor", lambda e, q_=q, ex=ex: e.transpose(
                pt2[0:8, q_ * 128:(q_ + 1) * 128], ex, identf[:, :]), "s_mm")
            bld.wait("scalar", "s_mm")
            bld.inc("scalar", lambda e, q_=q: e.copy(
                stats[0:8, q_ * 128:(q_ + 1) * 128], pt2[0:8, q_ * 128:(q_ + 1) * 128]), "s_sc")
            pp.rel(2, "s_sc")
            bld.wait("vector", "s_mm")
        pe_ = pp.acq(3)
        bld.wait("tensor", "s_sc")
        bld.inc("tensor", lambda e: e.matmul(pe_[:, :], expf[:, :], stats[0:8, 0:512],
                start=True, stop=True), "s_mm")
        bld.wait("scalar", "s_mm")
        bld.inc("scalar", lambda e: e.copy(scrA[:, 512:1024], pe_[:, :]), "s_sc")
        pp.rel(3, "s_sc")

        ph = pp.acq(0)
        groups = [(gview("A0T", g4 * 4, 4, 0, 128), 512) for g4 in range(8)]

        def use_a0(g, col):
            last = 0
            for t in range(4):
                k = g * 4 + t
                last = bld.inc("tensor", lambda e, k_=k, c=col + t * 128: e.matmul(
                    ph[:, :], dbw[:, c:c + 128], big[:, k_ * 512:(k_ + 1) * 512],
                    start=(k_ == 0), stop=(k_ == NT - 1)), "s_mm")
            return last
        stream2(groups, use_a0, dbw, 512)
        bld.wait("vector", "s_mm")
        bld.wait("vector", "s_sc")
        bld.inc("vector", lambda e: e.tensor_mul(scrA[:, 0:512], ph[:, :], scrA[:, 512:1024]), "s_ve")
        pp.rel(0, "s_ve")
        V_HW = bld.mark("s_ve")
        if DBG == "l0g" and sl == 0:
            bld.wait("sync", "s_ve")
            bld.wait("sync", "s_sc")
            scrAf = scrA.bitcast(F32)
            bld.dma("sync", xout[0:128, 0:512], scrAf[:, 0:512], "s_st")
            bld.dma("sync", xout[128:136, 0:512], stats[0:8, 0:512], "s_st")
            bld.dma("sync", xout[136:264, 0:64], scrB[:, 0:64], "s_st")
            bld.dma("sync", xout[264:392, 0:256], big.bitcast(F32)[:, 0:256], "s_st")
            bld.wait("sync", "s_st")
            return nc, _finalize(nc, bld)

        def epi_w0(mg, pb_, so_=so):
            for j in range(4):
                mt = mg * 4 + j
                bld.inc("vector", lambda e, j_=j, mt_=mt, pb=pb_: e.scalar_tensor_tensor(
                    out=x_t[:, mt_ * TLOC + so_: mt_ * TLOC + so_ + 512],
                    in0=pb[j_][:, :], scalar=bvec[:, mt_:mt_ + 1],
                    in1=x_t[:, mt_ * TLOC + so_: mt_ * TLOC + so_ + 512],
                    op0=ALU.add, op1=ALU.add), "s_ve")
        wchain_full("W0T", "B0S", scrA[:, 0:512], V_HW, epi_w0)

    if stage == 2:
        return nc, finish()

    bld.wait("sync", "s_cc", CC_WALLB)
    bld.wait("gpsimd", "s_cc", CC_WALLB)
    for sl in range(2):
        so = sl * 512
        ln_stats(so, RS[:, 0:512], RS[0:1, 0:512], MRS[:, 0:512], MRS[0:1, 0:512])
        for k in range(NT):
            xn_gen(k, so, k * 512, RS[:, 0:512], MRS[:, 0:512])
        V_XN = bld.mark("s_ve")
        bld.wait("tensor", "s_ve", V_XN)

        pss = pp.acq(2)
        qa_groups = [(gview("QAT", g4 * 4, 4, m * 128, (m + 1) * 128), 512)
                     for m in range(8) for g4 in range(8)]
        pqs = {}

        def use_qa(g, col):
            m, g8 = divmod(g, 8)
            if g8 == 0:
                pqs[m] = pp.acq(m % 2)
            pq_ = pqs[m]
            last = 0
            for t in range(4):
                k = g8 * 4 + t
                last = bld.inc("tensor", lambda e, k_=k, c=col + t * 128, pq=pq_: e.matmul(
                    pq[:, :], dbw[:, c:c + 128], big[:, k_ * 512:(k_ + 1) * 512],
                    start=(k_ == 0), stop=(k_ == NT - 1)), "s_mm")
            if g8 == 7:
                bld.wait("scalar", "s_mm")
                bld.inc("scalar", lambda e, m_=m, pq=pq_: e.copy(
                    s2[:, m_ * 512:(m_ + 1) * 512], pq[:, :]), "s_sc")
                bld.inc("scalar", lambda e, m_=m, pq=pq_: e.activation(
                    sqs[:, (m_ % 2) * 512:(m_ % 2) * 512 + 512], pq[:, :], AF.Square), "s_sc")
                bld.wait("tensor", "s_sc")
                last = bld.inc("tensor", lambda e, m_=m: e.matmul(
                    pss[0:1, :], onesb[:, 0:1], sqs[:, (m_ % 2) * 512:(m_ % 2) * 512 + 512],
                    start=(m_ == 0), stop=(m_ == 7)), "s_mm")
                pp.rel(m % 2, "s_sc")
            return last
        stream2(qa_groups, use_qa, dbw, 512)
        bld.wait("vector", "s_mm")
        A = stats[0:1, 0:512]
        bld.inc("vector", lambda e: e.tensor_scalar(A, pss[0:1, :], cst[0:1, C_I1024:C_I1024 + 1], None, ALU.mult), "s_ve")
        pp.rel(2, "s_ve")
        bld.wait("scalar", "s_ve")
        bld.inc("scalar", lambda e: e.activation(RS[0:1, 0:512], A, AF.Sqrt, bias=cst[0:1, C_EPS6:C_EPS6 + 1]), "s_sc")
        bld.wait("tensor", "s_sc")
        bcast(RS[0:1, 0:512], RS[:, 0:512], 3)
        bld.inc("vector", lambda e: e.reciprocal(RS[:, 0:512], RS[:, 0:512]), "s_ve")
        bld.wait("vector", "s_sc")
        ex_ev = [0, 0]
        for m in range(8):
            sl2 = (m % 2) * 512
            gs = "s_g0" if m % 2 == 0 else "s_g1"
            if m >= 2:
                bld.wait("vector", gs, ex_ev[m % 2])
            bld.inc("vector", lambda e, m_=m, s_=sl2: e.scalar_tensor_tensor(
                out=exch[:, s_:s_ + 512], in0=s2[:, m_ * 512:(m_ + 1) * 512],
                scalar=lnw[:, m_:m_ + 1], in1=RS[:, 0:512],
                op0=ALU.mult, op1=ALU.mult), "s_ve")
            bld.wait("gpsimd", "s_ve")
            ex_ev[m % 2] = bld.dma("gpsimd",
                    lat_in.ap()[m * 128:(m + 1) * 128, so:so + 512],
                    exch[:, sl2:sl2 + 512], gs)

        pss2 = pp.acq(2)
        KVM = [(128, 0), (128, 128), (64, 256)]
        kv_groups = [(gview("KAT", g4 * 4, 4, mof, mof + mw), 4 * mw)
                     for (mw, mof) in KVM for g4 in range(8)]
        pqs2 = {}

        def use_kv(g, col):
            mi, g8 = divmod(g, 8)
            mw = KVM[mi][0]
            if g8 == 0:
                pqs2[mi] = pp.acq(mi % 2)
            pq_ = pqs2[mi]
            last = 0
            for t in range(4):
                k = g8 * 4 + t
                last = bld.inc("tensor", lambda e, k_=k, c=col + t * mw, w_=mw, pq=pq_: e.matmul(
                    pq[0:w_, :], dbw[:, c:c + w_], big[:, k_ * 512:(k_ + 1) * 512],
                    start=(k_ == 0), stop=(k_ == NT - 1)), "s_mm")
            if g8 == 7:
                bld.wait("scalar", "s_mm")
                if mi < 2:
                    bld.inc("scalar", lambda e, m_=mi, pq=pq_: e.copy(
                        s2[:, 4096 + m_ * 512: 4096 + (m_ + 1) * 512], pq[:, :]), "s_sc")
                    bld.inc("scalar", lambda e, m_=mi, pq=pq_: e.activation(
                        sqs[:, (m_ % 2) * 512:(m_ % 2) * 512 + 512], pq[:, :], AF.Square), "s_sc")
                    bld.wait("tensor", "s_sc")
                    last = bld.inc("tensor", lambda e, m_=mi: e.matmul(
                        pss2[0:1, :], onesb[:, 0:1], sqs[:, (m_ % 2) * 512:(m_ % 2) * 512 + 512],
                        start=(m_ == 0), stop=(m_ == 1)), "s_mm")
                else:
                    bld.inc("scalar", lambda e, pq=pq_: e.copy(pe_scr[0:64, :], pq[0:64, :]), "s_sc")
                    bld.inc("scalar", lambda e, pq=pq_: e.copy(pe_scr[64:96, :], pq[32:64, :]), "s_sc")
                    bld.inc("scalar", lambda e, pq=pq_: e.copy(pe_scr[96:128, :], pq[0:32, :]), "s_sc")
                pp.rel(mi % 2, "s_sc")
            return last
        stream2(kv_groups, use_kv, dbw, 512)
        bld.wait("vector", "s_mm")
        bld.inc("vector", lambda e: e.tensor_scalar(A, pss2[0:1, :], cst[0:1, C_I256:C_I256 + 1], None, ALU.mult), "s_ve")
        pp.rel(2, "s_ve")
        bld.wait("scalar", "s_ve")
        bld.inc("scalar", lambda e: e.activation(MRS[0:1, 0:512], A, AF.Sqrt, bias=cst[0:1, C_EPS6:C_EPS6 + 1]), "s_sc")
        bld.wait("tensor", "s_sc")
        bcast(MRS[0:1, 0:512], MRS[:, 0:512], 3)
        bld.inc("vector", lambda e: e.reciprocal(MRS[:, 0:512], MRS[:, 0:512]), "s_ve")
        bld.wait("vector", "s_sc")
        for m in range(2):
            sl2 = (m % 2) * 512
            gs = "s_g0" if m % 2 == 0 else "s_g1"
            bld.wait("vector", gs, ex_ev[m % 2])
            bld.inc("vector", lambda e, m_=m, s_=sl2: e.scalar_tensor_tensor(
                out=exch[:, s_:s_ + 512], in0=s2[:, 4096 + m_ * 512: 4096 + (m_ + 1) * 512],
                scalar=lnw[:, 8 + m_:9 + m_], in1=MRS[:, 0:512],
                op0=ALU.mult, op1=ALU.mult), "s_ve")
            bld.wait("gpsimd", "s_ve")
            ex_ev[m % 2] = bld.dma("gpsimd",
                    lat_in.ap()[QLAT + m * 128:QLAT + (m + 1) * 128, so:so + 512],
                    exch[:, sl2:sl2 + 512], gs)
        bld.wait("vector", "s_sc")
        bld.wait("vector", "s_g0", ex_ev[0])
        bld.wait("vector", "s_g1", ex_ev[1])
        for bb in range(2):
            c0 = bb * 256
            bld.inc("vector", lambda e, c0=c0: e.tensor_mul(
                scrA[0:64, 0:256], pe_scr[0:64, c0:c0 + 256], CSL[0:64, :]), "s_ve")
            bld.inc("vector", lambda e, c0=c0: e.tensor_mul(
                scrA[0:64, 256:512], pe_scr[64:128, c0:c0 + 256], CSL[64:128, :]), "s_ve")
            bld.inc("vector", lambda e, c0=c0: e.tensor_sub(
                exch[0:32, c0:c0 + 256], scrA[0:32, 0:256], scrA[0:32, 256:512]), "s_ve")
            bld.inc("vector", lambda e, c0=c0: e.tensor_add(
                exch[32:64, c0:c0 + 256], scrA[32:64, 0:256], scrA[32:64, 256:512]), "s_ve")
        bld.wait("gpsimd", "s_ve")
        bld.dma("gpsimd", lat_in.ap()[QLAT + KVR:LATR, so:so + 512], exch[0:64, 0:512], "s_g0")

    bld.wait("gpsimd", "s_g0")
    bld.wait("gpsimd", "s_g1")
    bld.inc("gpsimd", lambda e: e.collective_compute(
        "AllGather", ALU.bypass, replica_groups=rg,
        ins=[lat_in.ap()], outs=[lat_g.ap()]), "s_cc", 1)
    CC_LAT = bld.mark("s_cc")

    bld.wait("sync", "s_cc", CC_LAT)
    exs = [0, 0]
    for b in range(B):
        bld.wait("sync", "s_mm")
        for k in range(2):
            bld.dma("sync", big[:, CKV + k * 2048: CKV + (k + 1) * 2048]
                    .rearrange("p (c d) -> p c d", c=8),
                    lat_v[0:8, QLAT + k * 128:QLAT + (k + 1) * 128,
                          b * 256:(b + 1) * 256].transpose([1, 0, 2]), "s_wS")
        bld.dma("sync", s2[0:64, 0:2048].rearrange("p (c d) -> p c d", c=8),
                lat_v[0:8, QLAT + KVR:LATR, b * 256:(b + 1) * 256].transpose([1, 0, 2]), "s_wS")
        CKV_LD = bld.mark("s_wS")
        for pair in range(2):
            for slab in range(4):
                bld.wait("sync", "s_mm")
                qa_ld = 0
                for k in range(8):
                    qa_ld = bld.dma("sync", big[:, QA + k * 512: QA + (k + 1) * 512]
                            .rearrange("p (c d) -> p c d", c=2),
                            lat_v[2 * slab:2 * slab + 2, k * 128:(k + 1) * 128,
                                  b * 256:(b + 1) * 256].transpose([1, 0, 2]), "s_wS")
                for hh in range(2):
                    hl = pair * 2 + hh
                    pq_ = pp.acq(5 + hh)
                    bld.wait("tensor", "s_wS", qa_ld)
                    for k in range(8):
                        bld.inc("tensor", lambda e, k_=k, hl_=hl, pq=pq_: e.matmul(
                            pq[:, :], wqb[:, k_ * 512 + hl_ * 128: k_ * 512 + (hl_ + 1) * 128],
                            big[:, QA + k_ * 512: QA + (k_ + 1) * 512],
                            start=(k_ == 0), stop=(k_ == 7)), "s_mm")
                    bld.wait("scalar", "s_mm")
                    qfc = QF + hh * 2048 + slab * 512
                    bld.inc("scalar", lambda e, pq=pq_, c=qfc: e.copy(big[0:64, c:c + 512], pq[0:64, :]), "s_sc")
                    bld.inc("scalar", lambda e, pq=pq_: e.copy(pe_scr[0:64, :], pq[64:128, :]), "s_sc")
                    bld.inc("scalar", lambda e, pq=pq_: e.copy(pe_scr[64:96, :], pq[96:128, :]), "s_sc")
                    bld.inc("scalar", lambda e, pq=pq_: e.copy(pe_scr[96:128, :], pq[64:96, :]), "s_sc")
                    pp.rel(5 + hh, "s_sc")
                    bld.wait("vector", "s_sc")
                    sc0 = slab * 512
                    bld.inc("vector", lambda e, s_=sc0: e.tensor_mul(
                        scrA[0:64, 0:512], pe_scr[0:64, :], CS2[0:64, s_:s_ + 512]), "s_ve")
                    bld.inc("vector", lambda e, s_=sc0: e.tensor_mul(
                        scrA[0:64, 512:1024], pe_scr[64:128, :], CS2[64:128, s_:s_ + 512]), "s_ve")
                    bld.inc("vector", lambda e, c=qfc: e.tensor_sub(
                        big[64:96, c:c + 512], scrA[0:32, 0:512], scrA[0:32, 512:1024]), "s_ve")
                    bld.inc("vector", lambda e, c=qfc: e.tensor_add(
                        big[96:128, c:c + 512], scrA[32:64, 0:512], scrA[32:64, 512:1024]), "s_ve")
                    bld.wait("scalar", "s_ve")
            bld.wait("tensor", "s_wS", CKV_LD)
            for slab in range(4):
                pk = pp.acq(5)
                for k in range(2):
                    bld.inc("tensor", lambda e, k_=k, s_=slab, pk=pk, pr=pair: e.matmul(
                        pk[:, :], wkvn[:, k_ * 256 + pr * 128: k_ * 256 + (pr + 1) * 128],
                        big[:, CKV + k_ * 2048 + s_ * 512: CKV + k_ * 2048 + (s_ + 1) * 512],
                        start=(k_ == 0), stop=(k_ == 1)), "s_mm")
                bld.wait("scalar", "s_mm")
                bld.inc("scalar", lambda e, s_=slab, pk=pk: e.copy(
                    big[0:64, KF + s_ * 512: KF + (s_ + 1) * 512], pk[0:64, :]), "s_sc")
                bld.inc("scalar", lambda e, s_=slab, pk=pk: e.copy(
                    big[0:64, KF + 2048 + s_ * 512: KF + 2048 + (s_ + 1) * 512], pk[64:128, :]), "s_sc")
                pp.rel(5, "s_sc")
            for hh in range(2):
                bld.inc("scalar", lambda e, hh_=hh: e.copy(
                    big[64:128, KF + hh_ * 2048: KF + (hh_ + 1) * 2048], s2[0:64, 0:2048]), "s_sc")
            for hh in range(2):
                hl = pair * 2 + hh
                for g4 in range(4):
                    pv = pp.acq(6)
                    for tc4 in range(4):
                        tc = g4 * 4 + tc4
                        for k in range(2):
                            bld.inc("tensor", lambda e, k_=k, t_=tc, j_=tc4, hl_=hl, pv=pv: e.matmul(
                                pv[:, j_ * 128:(j_ + 1) * 128],
                                big[:, CKV + k_ * 2048 + t_ * 128: CKV + k_ * 2048 + (t_ + 1) * 128],
                                wkvv[:, k_ * 512 + hl_ * 128: k_ * 512 + (hl_ + 1) * 128],
                                start=(k_ == 0), stop=(k_ == 1)), "s_mm")
                    bld.wait("scalar", "s_mm")
                    bld.inc("scalar", lambda e, hh_=hh, g_=g4, pv=pv: e.copy(
                        big[:, VT + hh_ * 2048 + g_ * 512: VT + hh_ * 2048 + (g_ + 1) * 512],
                        pv[:, :]), "s_sc")
                    pp.rel(6, "s_sc")
            if DBG == "qkv" and b == 0 and pair == 0:
                return nc, finish()
            bld.wait("tensor", "s_sc")
            for hh in range(2):
                hl = pair * 2 + hh
                pt_ev = [0, 0]
                ptv_ev = [0, 0]
                pscs, scm = {}, {}
                SCB = (0, 1, 4, 5)

                def em_score(sq_, t_, hh_=hh, pscs=pscs, scm=scm):
                    pscs[(sq_, t_)] = pp.acq(SCB[t_ % 4])
                    scm[(sq_, t_)] = bld.inc("tensor", lambda e, t_=t_, s_=sq_, hh_=hh_,
                                             psc=pscs[(sq_, t_)]: e.matmul(
                        psc[:, :], big[:, KF + hh_ * 2048 + t_ * 128: KF + hh_ * 2048 + (t_ + 1) * 128],
                        big[:, QF + hh_ * 2048 + s_ * 512: QF + hh_ * 2048 + (s_ + 1) * 512],
                        start=True, stop=True), "s_mm")
                em_score(0, 0)
                em_score(0, 1)
                for sq in range(4):
                    pav = pp.acq(2)
                    psm = pp.acq(3)
                    nt_ = 4 * sq + 4
                    for t in range(nt_):
                        if t + 2 < nt_:
                            em_score(sq, t + 2)
                        elif sq < 3:
                            em_score(sq + 1, t + 2 - nt_)
                        psc = pscs.pop((sq, t))
                        bld.wait("scalar", "s_mm", scm.pop((sq, t)))
                        bld.wait("scalar", "s_mm", pt_ev[t % 2])
                        bld.wait("scalar", "s_ve", ptv_ev[t % 2])
                        bld.inc("scalar", lambda e, t_=t, psc=psc: e.activation(
                            Pt[:, (t_ % 2) * 512:(t_ % 2) * 512 + 512], psc[:, :], AF.Exp,
                            bias=cst[:, C_NEG30:C_NEG30 + 1], scale=cst[:, C_SMS:C_SMS + 1]), "s_sc")
                        pp.rel(SCB[t % 4], "s_sc")
                        exp_mark = bld.mark("s_sc")
                        d = t - 4 * sq
                        bld.wait("vector", "s_sc", exp_mark)
                        if d >= 0:
                            if d > 0:
                                bld.inc("vector", lambda e, t_=t, d_=d: e.memset(
                                    Pt[:, (t_ % 2) * 512:(t_ % 2) * 512 + 128 * d_], 0.0), "s_ve")
                            bld.inc("vector", lambda e, t_=t, d_=d: e.tensor_mul(
                                Pt[:, (t_ % 2) * 512 + 128 * d_:(t_ % 2) * 512 + 128 * (d_ + 1)],
                                Pt[:, (t_ % 2) * 512 + 128 * d_:(t_ % 2) * 512 + 128 * (d_ + 1)],
                                trib[:, :]), "s_ve")
                        trib_mark = bld.mark("s_ve")
                        if t == 0:
                            bld.inc("vector", lambda e: e.tensor_copy(
                                stats[:, 0:512], Pt[:, 0:512]), "s_ve")
                        else:
                            bld.inc("vector", lambda e, t_=t: e.tensor_add(
                                stats[:, 0:512], stats[:, 0:512],
                                Pt[:, (t_ % 2) * 512:(t_ % 2) * 512 + 512]), "s_ve")
                        ptv_ev[t % 2] = bld.mark("s_ve")
                        if d >= 0:
                            bld.wait("tensor", "s_ve", trib_mark)
                        else:
                            bld.wait("tensor", "s_sc", exp_mark)
                        pt_ev[t % 2] = bld.inc("tensor", lambda e, t_=t, hh_=hh, pav=pav, n_=nt_: e.matmul(
                            pav[:, :], big[:, VT + hh_ * 2048 + t_ * 128: VT + hh_ * 2048 + (t_ + 1) * 128],
                            Pt[:, (t_ % 2) * 512:(t_ % 2) * 512 + 512],
                            start=(t_ == 0), stop=(t_ == n_ - 1)), "s_mm")
                    bld.wait("tensor", "s_ve")
                    bld.inc("tensor", lambda e, psm=psm: e.matmul(
                        psm[0:1, :], onesf[:, 0:1], stats[:, 0:512],
                        start=True, stop=True), "s_mm")
                    bld.wait("scalar", "s_mm")
                    bld.inc("scalar", lambda e, psm=psm: e.copy(MRS[0:1, 0:512], psm[0:1, :]), "s_sc")
                    pp.rel(3, "s_sc")
                    bld.wait("tensor", "s_sc")
                    bcast(MRS[0:1, 0:512], MRS[:, 0:512], 4)
                    bld.inc("vector", lambda e: e.reciprocal(MRS[:, 0:512], MRS[:, 0:512]), "s_ve")
                    bld.wait("vector", "s_sc")
                    es = sq % 2
                    bld.wait("vector", "s_g0" if es == 0 else "s_g1", exs[es])
                    bld.inc("vector", lambda e, pav=pav, es=es: e.tensor_mul(
                        exch[:, es * 512:es * 512 + 512], pav[:, :], MRS[:, 0:512]), "s_ve")
                    pp.rel(2, "s_ve")
                    bld.wait("gpsimd", "s_ve")
                    exs[es] = bld.dma("gpsimd",
                            atin_v[2 * sq:2 * sq + 2, hl * 128:(hl + 1) * 128,
                                   b * 256:(b + 1) * 256].transpose([1, 0, 2]),
                            exch[:, es * 512:es * 512 + 512].rearrange("p (c d) -> p c d", c=2),
                            "s_g0" if es == 0 else "s_g1")

    bld.wait("gpsimd", "s_g0")
    bld.wait("gpsimd", "s_g1")

    def _a2a(e):
        with allow_ici_all_to_all():
            return e.collective_compute(
                "AllToAll", ALU.bypass, replica_groups=rg,
                ins=[at_in.ap()], outs=[at_a2a.ap()])
    bld.inc("gpsimd", _a2a, "s_cc", 1)
    CC_A2A = bld.mark("s_cc")

    bld.wait("sync", "s_cc", CC_A2A)
    for sl in range(2):
        so = sl * 512
        bld.wait("sync", "s_mm")
        ao_ld = bld.dma("sync", big[:, 0:16384].rearrange("p (g w) -> p g w", g=NT),
                at_a2a.ap()[:, so:so + 512].rearrange("(g p) w -> p g w", p=128), "s_wS")
        bld.wait("tensor", "s_wS", ao_ld)

        def epi_wo(mg, pb_, so_=so):
            for j in range(4):
                mt = mg * 4 + j
                bld.inc("vector", lambda e, j_=j, mt_=mt, pb=pb_: e.tensor_add(
                    x_t[:, mt_ * TLOC + so_: mt_ * TLOC + so_ + 512], pb[j_][:, :],
                    x_t[:, mt_ * TLOC + so_: mt_ * TLOC + so_ + 512]), "s_ve")
        wchain_full("WOT", None, None, None, epi_wo)

    if stage == 3:
        return nc, finish()

    for sl in range(2):
        so = sl * 512
        ln_stats(so, RS[:, 0:512], RS[0:1, 0:512], MRS[:, 0:512], MRS[0:1, 0:512])
        if sl == 0:
            bld.inc("vector", lambda e: e.tensor_copy(stats[32:33, 0:512], RS[0:1, 0:512]), "s_ve")
            bld.inc("vector", lambda e: e.tensor_copy(stats[64:65, 0:512], MRS[0:1, 0:512]), "s_ve")
        for k in range(NT):
            xn_gen(k, so, k * 512, RS[:, 0:512], MRS[:, 0:512])
        for k in range(NT):
            for bb in range(2):
                gb = sl * 2 + bb
                bld.inc("vector", lambda e, k_=k, bb_=bb, g_=gb: e.scalar_tensor_tensor(
                    out=sqs[:, 0:256], in0=big[:, k_ * 512 + bb_ * 256: k_ * 512 + bb_ * 256 + 256],
                    scalar=1.0, in1=maskb[:, 0:256], op0=ALU.mult, op1=ALU.mult,
                    accum_out=rpart[:, k_ * 4 + g_: k_ * 4 + g_ + 1]), "s_ve")
    bld.wait("gpsimd", "s_ve")
    bld.dma("gpsimd", rt_in.ap(), rpart[:, :], "s_gc")
    bld.wait("gpsimd", "s_gc")
    bld.inc("gpsimd", lambda e: e.collective_compute(
        "AllReduce", ALU.add, replica_groups=rg,
        ins=[rt_in.ap()], outs=[rt_out.ap()]), "s_cc", 1)
    bld.wait("gpsimd", "s_cc")
    bld.dma("gpsimd", rpart[:, :], rt_out.ap(), "s_gc")
    bld.wait("vector", "s_gc")
    bld.inc("vector", lambda e: e.tensor_scalar(Pt[:, 0:128], rpart[:, :],
            cst[:, C_IPOOL:C_IPOOL + 1], None, ALU.mult), "s_ve")
    bld.dma("sync", dba[:, :].rearrange("p (k e) -> p k e", k=32),
            wv("RWT", 0, D, 0, 8).rearrange("(k p) e -> p k e", p=128), "s_wS")
    pl = pp.acq(1)
    bld.wait("tensor", "s_wS")
    bld.wait("tensor", "s_ve")
    for k in range(NT):
        bld.inc("tensor", lambda e, k_=k: e.matmul(
            pl[0:8, 0:4], dba[:, k_ * 8:(k_ + 1) * 8], Pt[:, k_ * 4:(k_ + 1) * 4],
            start=(k_ == 0), stop=(k_ == NT - 1)), "s_mm")
    DBA_FREE2 = bld.mark("s_mm")
    bld.wait("scalar", "s_mm")
    bld.inc("scalar", lambda e: e.activation(scrB[0:8, 0:4], pl[0:8, 0:4], AF.Identity,
            bias=rbt[:, 0:1]), "s_sc")
    pp.rel(1, "s_sc")
    pt_ = pp.acq(2)
    bld.wait("tensor", "s_sc")
    bld.inc("tensor", lambda e: e.transpose(pt_[0:4, 0:8], scrB[0:8, 0:4], identf[0:8, 0:8]), "s_mm")
    bld.wait("scalar", "s_mm")
    bld.inc("scalar", lambda e: e.copy(scrB[0:4, 8:16], pt_[0:4, 0:8]), "s_sc")
    pp.rel(2, "s_sc")
    bld.wait("vector", "s_sc")
    vops(lambda e: e.reduce_max(scrB[0:4, 16:17], scrB[0:4, 8:16], AX),
         lambda e: e.tensor_scalar(scrB[0:4, 17:18], scrB[0:4, 16:17], -1.0, None, ALU.mult))
    bld.wait("scalar", "s_ve")
    bld.inc("scalar", lambda e: e.activation(scrB[0:4, 8:16], scrB[0:4, 8:16], AF.Exp,
            bias=scrB[0:4, 17:18]), "s_sc")
    bld.wait("scalar", "s_sc")
    bld.wait("vector", "s_sc")
    vops(lambda e: e.reduce_sum(scrB[0:4, 16:17], scrB[0:4, 8:16], AX),
         lambda e: e.reciprocal(scrB[0:4, 16:17], scrB[0:4, 16:17]),
         lambda e: e.tensor_scalar(scrB[0:4, 8:16], scrB[0:4, 8:16],
            scrB[0:4, 16:17], None, ALU.mult))
    pt_ = pp.acq(2)
    bld.wait("tensor", "s_ve")
    bld.inc("tensor", lambda e: e.transpose(pt_[0:8, 0:4], scrB[0:4, 8:16], identf[0:4, 0:4]), "s_mm")
    bld.wait("scalar", "s_mm")
    bld.inc("scalar", lambda e: e.copy(scrB[0:8, 20:24], pt_[0:8, 0:4]), "s_sc")
    pp.rel(2, "s_sc")
    pe_ = pp.acq(3)
    bld.wait("tensor", "s_sc")
    bld.inc("tensor", lambda e: e.matmul(pe_[:, 0:4], expf[:, :], scrB[0:8, 20:24],
            start=True, stop=True), "s_mm")
    bld.wait("scalar", "s_mm")
    bld.inc("scalar", lambda e: e.activation(scrB[:, 24:28], pe_[:, 0:4], AF.Identity,
            scale=cst[:, C_TWO:C_TWO + 1]), "s_sc")
    pp.rel(3, "s_sc")
    bld.wait("vector", "s_sc")

    for sl in (1, 0):
        so = sl * 512
        if sl == 0:
            bcast(stats[32:33, 0:512], RS[:, 0:512], 2, lhs=ones3f[32:33, :])
            bcast(stats[64:65, 0:512], MRS[:, 0:512], 3, lhs=ones3f[64:65, :])
            for k in range(NT):
                xn_gen(k, so, k * 512, RS[:, 0:512], MRS[:, 0:512])
        V_XN = bld.mark("s_ve")
        bld.wait("tensor", "s_ve", V_XN)
        ph = pp.acq(0)
        groups = [(gview("A1T", g4 * 4, 4, 0, 128), 512) for g4 in range(8)]

        def use_a1(g, col, ph=ph):
            last = 0
            for t in range(4):
                k = g * 4 + t
                last = bld.inc("tensor", lambda e, k_=k, c=col + t * 128: e.matmul(
                    ph[:, :], dbw[:, c:c + 128], big[:, k_ * 512:(k_ + 1) * 512],
                    start=(k_ == 0), stop=(k_ == NT - 1)), "s_mm")
            return last
        stream2(groups, use_a1, dbw, 512)
        bld.wait("vector", "s_mm")
        for bb in range(2):
            gb = sl * 2 + bb
            bld.inc("vector", lambda e, bb_=bb, g_=gb, ph=ph: e.tensor_scalar(
                scrA[:, bb_ * 256:(bb_ + 1) * 256], ph[:, bb_ * 256:(bb_ + 1) * 256],
                scrB[:, 24 + g_:25 + g_], None, ALU.mult), "s_ve")
        pp.rel(0, "s_ve")
        V_HW = bld.mark("s_ve")

        def epi_w1(mg, pb_, so_=so):
            for j in range(4):
                mt = mg * 4 + j
                bld.inc("vector", lambda e, j_=j, mt_=mt, pb=pb_: e.scalar_tensor_tensor(
                    out=x_t[:, mt_ * TLOC + so_: mt_ * TLOC + so_ + 512],
                    in0=pb[j_][:, :], scalar=bvec[:, 32 + mt_:33 + mt_],
                    in1=x_t[:, mt_ * TLOC + so_: mt_ * TLOC + so_ + 512],
                    op0=ALU.add, op1=ALU.add), "s_ve")
        wchain_full("W1T", "B1S", scrA[:, 0:512], V_HW, epi_w1)

    return nc, finish()


def _finalize(nc, bld):
    ctx = ExitStack()
    for name in sorted(bld.plan):
        bld.sems[name] = ctx.enter_context(nc.semaphore(name))
    with nc.Block() as block:
        for eng in Bld.ENGS:
            def section(e, _ops=tuple(bld.ops[eng])):
                for fn in _ops:
                    fn(e)
            getattr(block, eng)(section)
    ctx.close()
    return True


_CACHE = {}


def kernel(**inputs):
    in_maps, offs, wps = _prep_host(inputs)
    key = (wps, STAGE)
    if key not in _CACHE:
        nc, _ = build_program(offs, wps, STAGE)
        _CACHE[key] = nc
    nc = _CACHE[key]
    res = run_bass_kernel_spmd(nc, in_maps, core_ids=list(range(NCORES)))
    out = np.empty((B, S, D), np.float32)
    for c in range(NCORES):
        xo = res.results[c]["xout"]
        out[:, SCH * c:SCH * (c + 1), :] = xo.T.reshape(B, SCH, D)
    return out
